# version 05
# baseline (speedup 1.0000x reference)
import numpy as np
from contextlib import ExitStack

import concourse.bass as bass
import concourse.mybir as mybir

F32 = mybir.dt.float32
BF16 = mybir.dt.bfloat16
I32 = mybir.dt.int32
AF = mybir.ActivationFunctionType
ALU = mybir.AluOpType
AX = mybir.AxisListType

C = 2048
T = 1024
L = 64
NCH = T // L
KC = C // 128
HID = 8192
MC = HID // 128
NCORES = 8
GEPS = float(np.float32(1e-5 * 8.0 ** 2))
LEPS = 1e-5
NTT = T // 128
WQ = 3 * 1024 * C
WKV_EMAX = None


class Sem:
    def __init__(self, nc, es, name):
        self.h = es.enter_context(nc.semaphore(name))
        self.n = 0

    def inc(self, instr, k=1):
        instr.then_inc(self.h, k)
        self.n += k
        return self.n


def entry_waits(eng, b, s0):
    for k in ("te", "ve", "se"):
        if s0[k]:
            eng.wait_ge(b.sm[k].h, s0[k])


def snap(b):
    return {k: s.n for k, s in b.sm.items()}


def build(stop_after=None, dumps=()):
    nc = bass.Bass("TRN2", target_bir_lowering=False, debug=False,
                   detect_race_conditions=False)
    es = ExitStack()
    b = type("B", (), {})()
    b.nc, b.es, b.dumps = nc, es, set(dumps)
    b.extra_outs = []
    di = {}

    def dram_in(name, shape, dt=F32):
        di[name] = nc.dram_tensor(name, shape, dt, kind="ExternalInput").ap()

    b.di = di
    dram_in("x_fm", [C, T])
    dram_in("xprev", [128, KC])
    dram_in("x_tm", [T, C])
    for w in ("r", "k", "v", "g", "o"):
        dram_in(f"W{w}T", [C, C])
    dram_in("maa_w1", [C, 160])
    dram_in("maa_w2", [5, 32, C])
    dram_in("dw1", [C, 64])
    dram_in("dw2", [64, C])
    dram_in("maa_vecs", [128, KC * 6])
    dram_in("td_vec", [128, KC])
    dram_in("u_vec", [128, KC])
    dram_in("lng", [128, KC])
    dram_in("lnb", [128, KC])
    dram_in("ident_bf", [128, 128], BF16)
    dram_in("ident_f32", [128, 128], F32)
    dram_in("masksel", [128, 4 * L])
    dram_in("maskdiag_u8", [128, 2 * L], mybir.dt.uint8)
    dram_in("e_all", [128, KC * 32], BF16)
    dram_in("f_all", [32, KC * 128], BF16)
    dram_in("ones_f32", [128, 512])
    dram_in("epsv", [128, 2])
    dram_in("m2aug", [9, 8])
    dram_in("gate_slice", [C, 1024])
    dram_in("val_slice", [C, 1024])
    dram_in("out_slice", [1024, C])
    dram_in("gate_b", [128, MC])
    dram_in("val_b", [128, MC])
    dram_in("out_b", [4, 512])

    b.out_hbm = nc.dram_tensor("out", [T, C], F32, kind="ExternalOutput").ap()
    b.x1_scr = nc.dram_tensor("x1_scr", [T, C], F32).ap()
    b.cc_s_in = nc.dram_tensor("cc_s_in", [1, 128 * T + C], F32)
    b.cc_s_out = nc.dram_tensor("cc_s_out", [NCORES, 128 * T + C], F32)
    b.cc_gam_in = nc.dram_tensor("cc_gam_in", [1, 4], F32)
    b.cc_gam_out = nc.dram_tensor("cc_gam_out", [1, 4], F32)
    b.cc_wq_in = nc.dram_tensor("cc_wq_in", [1, WQ], BF16)
    b.cc_wq_out = nc.dram_tensor("cc_wq_out", [NCORES, WQ], BF16)

    S = lambda name: Sem(nc, es, name)
    b.sm = dict(dxa=S("dxa"), dxb=S("dxb"), dwa=S("dwa"), dwb=S("dwb"), dc=S("dc"),
                dga=S("dga"), dgb=S("dgb"), dq=S("dq"), st=S("st"), te=S("te"),
                ve=S("ve"), se=S("se"), ge=S("ge"), cc=S("ccs"), vq=S("vq"),
                dfa=S("dfa"), dfb=S("dfb"), tdr=S("tdr"))

    b.sb = lambda name, shape, dt=F32: es.enter_context(nc.sbuf_tensor(name, shape, dt))
    b.block = es.enter_context(nc.Block())
    b._stacks = []
    emit(b, stop_after)
    for s in reversed(b._stacks):
        s.close()
    es.close()
    return nc, b


def psum(b, scope, name, shape, dt=F32):
    return scope.enter_context(b.nc.psum_tensor(name, shape, dt))


def emit(b, stop_after):
    nc, block, sm = b.nc, b.block, b.sm
    sb = b.sb

    cst = b.cst = {}
    pairs = [("maa", "maa_vecs", [128, KC * 6], F32), ("td", "td_vec", [128, KC], F32),
             ("u", "u_vec", [128, KC], F32), ("lng", "lng", [128, KC], F32),
             ("lnb", "lnb", [128, KC], F32), ("ident", "ident_bf", [128, 128], BF16),
             ("identf", "ident_f32", [128, 128], F32),
             ("masksel", "masksel", [128, 4 * L], F32),
             ("maskd8", "maskdiag_u8", [128, 2 * L], mybir.dt.uint8),
             ("e_all", "e_all", [128, KC * 32], BF16),
             ("ones", "ones_f32", [128, 512], F32), ("epsv", "epsv", [128, 2], F32),
             ("m2aug", "m2aug", [9, 8], F32),
             ("xprev", "xprev", [128, KC], F32),
             ("gate_b", "gate_b", [128, MC], F32), ("val_b", "val_b", [128, MC], F32)]
    for nm, src, shape, dt in pairs:
        cst[nm] = sb("c_" + nm, shape, dt)

    @block.sync
    def _(sy):
        for nm, src, shape, dt in pairs:
            sm["dc"].inc(sy.dma_start(out=cst[nm][:, :], in_=b.di[src][:, :]), 16)

    if stop_after == "consts":
        return fin(b)
    b.mixA = sb("mixA", [128, KC * T], BF16)
    b.ctot = sb("ctot", [128, KC])
    b.tmp0 = sb("tmp0", [128, T])
    b.tmp1 = sb("tmp1", [128, T])
    b.tmp2 = sb("tmp2", [128, T])
    b.gamsb = sb("gamsb", [1, 4])
    b.thr = sb("thr", [128, 8])
    b.thrn = sb("thrn", [128, 8])
    b.sx1 = sb("sx1", [128, NTT])
    b.s3v = sb("s3v", [128, NTT])
    b.junk = sb("junk", [128, 1])
    b.vnop = sb("vnop", [1, 2])
    b.csc = ExitStack()
    b._stacks.append(b.csc)
    b.cstats = b.csc.enter_context(nc.sbuf_tensor("cstats", [128, KC * NCH * 6], F32))
    b.xxt = b.csc.enter_context(nc.sbuf_tensor("xxt", [128, T], BF16))

    b.att = ExitStack()
    b._stacks.append(b.att)
    A = lambda name, shape, dt=F32: b.att.enter_context(nc.sbuf_tensor(name, shape, dt))
    b.early = ExitStack()
    b._stacks.append(b.early)
    E = lambda name, shape, dt=F32: b.early.enter_context(nc.sbuf_tensor(name, shape, dt))
    b.a_pack = E("a_pack", [128, T], BF16)
    b.a4 = E("a4", [32, T], BF16)
    b.w2s = E("w2s", [128, 2 * C], BF16)
    b.early2 = ExitStack()
    b._stacks.append(b.early2)
    E2 = lambda name, shape, dt=F32: b.early2.enter_context(
        nc.sbuf_tensor(name, shape, dt, side="right"))
    b.w1s = E2("w1s", [128, KC * 160], BF16)
    b.dw1s = E2("dw1s", [128, KC * 64], BF16)
    b.dw2s = E2("dw2s", [64, C], BF16)
    b.tanhw = E2("tanhw", [64, T], BF16)
    b.cscope_a = ExitStack()
    b.cscope_b = ExitStack()
    b._stacks.append(b.cscope_b)
    b._stacks.append(b.cscope_a)
    b.cb_b = b.cscope_b.enter_context(nc.sbuf_tensor("cbuf_b", [128, 8 * T], F32))
    b.cb_a = b.cscope_a.enter_context(nc.sbuf_tensor("cbuf_a", [128, 8 * T], F32))

    ph_x_xx_xxx(b)
    if stop_after == "xx":
        return fin(b)
    ph_a(b, sub=stop_after if stop_after in ("a1", "a2") else None)
    if stop_after in ("a", "a1", "a2"):
        return fin(b)
    ph_decay(b)
    b.early2.close()
    if stop_after == "decay":
        return fin(b)
    b.bsc = ExitStack()
    b._stacks.append(b.bsc)
    R = lambda name, w: b.bsc.enter_context(nc.sbuf_tensor(name, [128, w], BF16, side="right"))
    b.rt = R("rt", KC * T)
    ph_mix(b, 3, "r", b.mixA)
    ph_proj_fm(b, "r", lambda o: b.rt[:, o * T:(o + 1) * T], incl=False)
    ph_mix(b, 1, "k", b.mixA)
    b.kt_a = R("kt_a", 8 * T)
    b.yd_a = R("yd_a", 8 * T)
    ph_proj_fm(b, "k", lambda o: ktc(b, o), incl=True, orange=(0, 8))
    b.cscope_a.close()
    b.kt_b = R("kt_b", 8 * T)
    b.yd_b = R("yd_b", 8 * T)
    ph_proj_fm(b, "k", lambda o: ktc(b, o), incl=True, orange=(8, 16))
    b.cscope_b.close()
    dump(b, "uk_a", b.yd_a[:, :])
    dump(b, "uk_b", b.yd_b[:, :])
    dump(b, "kt_a", b.kt_a[:, :])
    dump(b, "rt", b.rt[:, :])
    if stop_after == "kr":
        return fin(b)
    ph_wq_sums(b)
    ph_wq_ternary(b)
    if stop_after == "wq":
        return fin(b)
    ph_mix(b, 2, "v", b.mixA)
    b.vsc = ExitStack()
    b._stacks.append(b.vsc)
    b.vt = b.vsc.enter_context(nc.sbuf_tensor("vt", [128, NTT * C], BF16, side="right"))
    ph_vproj(b)
    dump(b, "vt", b.vt[:, :])
    ph_mix(b, 4, "g", b.mixA)
    b.early.close()
    if stop_after == "prewkv":
        return fin(b)
    b.wsc = ExitStack()
    b._stacks.append(b.wsc)
    W = lambda name, shape, dt=F32: b.wsc.enter_context(nc.sbuf_tensor(name, shape, dt))
    b.Spack = W("Spack", [128, T], F32)
    b.s0s = W("s0s", [128, L], BF16)
    b.khf = W("khf", [128, L], BF16)
    b.kht = W("kht", [128, 2 * L], BF16)
    b.asel = W("asel", [128, 2 * L], BF16)
    b.tsel = W("tsel", [128, 2 * L], BF16)
    b.ctrow = W("ctrow", [16, 128], F32)
    ph_wkv(b)
    if stop_after == "wkv":
        return fin(b)
    ph_publish(b)
    ph_gproj(b)
    b.vsc.close()
    b.wsc.close()
    ph_combine_correct(b)
    b.csc.close()
    if stop_after == "corr":
        return fin(b)
    ph_gn(b)
    dump(b, "yng_a", b.yd_a[:, :])
    dump(b, "yng_b", b.yd_b[:, :])
    if stop_after == "gn":
        return fin(b)
    ph_wo_ln1(b)
    if stop_after == "x1":
        return fin(b)
    b.bsc.close()
    b.att.close()
    ph_ffn(b)
    fin(b)


def fin(b):
    block, sm = b.block, b.sm
    tgt = {k: s.n for k, s in sm.items()}

    @block.sync
    def _(sy):
        for k, v in tgt.items():
            if v:
                sy.wait_ge(sm[k].h, v)

    @block.gpsimd
    def _(g):
        g.wait_ge(sm["st"].h, sm["st"].n)


def dump(b, name, ap):
    if name not in b.dumps:
        return
    d = b.nc.dram_tensor("dump_" + name, list(ap.shape), ap.dtype, kind="ExternalOutput").ap()
    b.extra_outs.append("dump_" + name)
    sm = b.sm
    tgt = {k: s.n for k, s in sm.items()}

    @b.block.sync
    def _(sy):
        for k, v in tgt.items():
            if v:
                sy.wait_ge(sm[k].h, v)
        sm["st"].inc(sy.dma_start(out=d, in_=ap), 16)



def ph_x_xx_xxx(b):
    block, sm, cst = b.block, b.sm, b.cst
    s0 = snap(b)
    dxs = [sm["dxa"], sm["dxb"]]
    sc = ExitStack()
    xst = [psum(b, sc, "xs0", [128, T]) if False else
           sc.enter_context(b.nc.sbuf_tensor("xsA0", [128, T], F32)),
           sc.enter_context(b.nc.sbuf_tensor("xsA1", [128, T], F32))]

    @block.sync
    def _(sy):
        sy.wait_ge(sm["dc"].h, s0["dc"])
        for i in range(KC):
            if i >= 2:
                sy.wait_ge(sm["ve"].h, s0["ve"] + 3 * (i - 1))
            dxs[i % 2].inc(sy.dma_start(out=xst[i % 2][:, :],
                                        in_=b.di["x_fm"][i * 128:(i + 1) * 128, :]), 16)

    @block.vector
    def _(v):
        for i in range(KC):
            x = xst[i % 2]
            v.wait_ge(dxs[i % 2].h, s0["dxa" if i % 2 == 0 else "dxb"] + 16 * (i // 2 + 1))
            v.tensor_tensor(b.xxt[:, 0:1], cst["xprev"][:, i:i + 1], x[:, 0:1], ALU.subtract)
            v.tensor_tensor(b.xxt[:, 1:T], x[:, 0:T - 1], x[:, 1:T], ALU.subtract)
            sm["ve"].inc(v.scalar_tensor_tensor(
                b.mixA[:, i * T:(i + 1) * T], b.xxt[:, :], cst["maa"][:, i * 6:i * 6 + 1],
                x[:, :], ALU.mult, ALU.add), 3)

    sc.close()


def ph_a(b, sub=None):
    nc, block, sm = b.nc, b.block, b.sm
    s0 = snap(b)

    @block.gpsimd
    def _(g):
        sm["dga"].inc(g.dma_start(
            out=b.w1s[:, :].rearrange("p (k f) -> p k f", f=160),
            in_=b.di["maa_w1"].rearrange("(k p) f -> p k f", p=128)), 16)
        sm["dga"].inc(g.dma_start(
            out=b.w2s[:, 0:C],
            in_=b.di["maa_w2"][0:4].rearrange("f d c -> (f d) c")), 16)
        sm["dga"].inc(g.dma_start(out=b.w2s[0:32, C:2 * C], in_=b.di["maa_w2"][4]), 16)
        sm["dga"].inc(g.dma_start(
            out=b.dw1s[:, :].rearrange("p (k f) -> p k f", f=64),
            in_=b.di["dw1"].rearrange("(k p) f -> p k f", p=128)), 16)
        sm["dga"].inc(g.dma_start(out=b.dw2s[:, :], in_=b.di["dw2"][:, :]), 16)

    if sub == "a1":
        return

    scope = ExitStack()
    pa = psum(b, scope, "pa", [32, T])

    @block.tensor
    def _(t):
        t.wait_ge(sm["ve"].h, s0["ve"])
        t.wait_ge(sm["dga"].h, s0["dga"] + 16)
        for f in range(5):
            if f >= 1 and sub != "a2":
                t.wait_ge(sm["se"].h, s0["se"] + f)
            for k in range(KC):
                lw = b.w1s[:, k * 160 + f * 32: k * 160 + (f + 1) * 32]
                for h in range(2):
                    mm = t.matmul(pa[:, h * 512:(h + 1) * 512], lw,
                                  b.mixA[:, k * T + h * 512: k * T + (h + 1) * 512],
                                  start=(k == 0), stop=(k == KC - 1))
            sm["te"].inc(mm, 1)

    if sub == "a2":
        scope.close()
        return

    @block.scalar
    def _(s):
        for f in range(5):
            s.wait_ge(sm["te"].h, s0["te"] + f + 1)
            dst = b.a_pack[32 * f:32 * (f + 1), :] if f < 4 else b.a4[:, :]
            sm["se"].inc(s.activation(dst, pa[:, :], AF.Tanh), 1)

    scope.close()
    dump(b, "a_pack", b.a_pack[:, :])


def ph_mix(b, fidx, P, dst):
    nc, block, sm, cst = b.nc, b.block, b.sm, b.cst
    s0 = snap(b)
    midx = {"x": 0, "w": 1, "k": 2, "v": 3, "r": 4, "g": 5}[P]
    dxs = [sm["dxa"], sm["dxb"]]
    scope = ExitStack()
    pm = [psum(b, scope, f"pm{P}0", [128, T]), psum(b, scope, f"pm{P}1", [128, T])]
    xst = [scope.enter_context(nc.sbuf_tensor(f"xs{P}0", [128, T], F32)),
           scope.enter_context(nc.sbuf_tensor(f"xs{P}1", [128, T], F32))]

    @block.sync
    def _(sy):
        entry_waits(sy, b, s0)
        for i in range(KC):
            if i >= 2:
                sy.wait_ge(sm["ve"].h, s0["ve"] + 2 * (i - 1))
            dxs[i % 2].inc(sy.dma_start(out=xst[i % 2][:, :],
                                        in_=b.di["x_fm"][i * 128:(i + 1) * 128, :]), 16)

    @block.tensor
    def _(t):
        t.wait_ge(sm["se"].h, s0["se"])
        t.wait_ge(sm["ve"].h, s0["ve"])
        for i in range(KC):
            if i >= 2:
                t.wait_ge(sm["ve"].h, s0["ve"] + 2 * (i - 1))
            if fidx < 4:
                tp = (32 * fidx, 0)
                rhs = b.a_pack[32 * fidx:32 * (fidx + 1), :]
                lw = b.w2s[32 * fidx:32 * (fidx + 1), i * 128:(i + 1) * 128]
            else:
                tp = (0, 0)
                rhs = b.a4[:, :]
                lw = b.w2s[0:32, C + i * 128: C + (i + 1) * 128]
            for h in range(2):
                mm = t.matmul(pm[i % 2][:, h * 512:(h + 1) * 512], lw,
                              rhs[:, h * 512:(h + 1) * 512],
                              start=True, stop=True, tile_position=tp)
            sm["te"].inc(mm, 1)

    @block.vector
    def _(v):
        for i in range(KC):
            v.wait_ge(sm["te"].h, s0["te"] + i + 1)
            v.wait_ge(dxs[i % 2].h, s0["dxa" if i % 2 == 0 else "dxb"] + 16 * (i // 2 + 1))
            x = xst[i % 2]
            v.tensor_tensor(b.xxt[:, 0:1], cst["xprev"][:, i:i + 1], x[:, 0:1], ALU.subtract)
            v.tensor_tensor(b.xxt[:, 1:T], x[:, 0:T - 1], x[:, 1:T], ALU.subtract)
            sm["ve"].inc(v.scalar_tensor_tensor(
                b.tmp0[:, :], pm[i % 2][:, :],
                cst["maa"][:, i * 6 + midx: i * 6 + midx + 1],
                b.xxt[:, :], ALU.add, ALU.mult), 1)
            sm["ve"].inc(v.tensor_tensor(dst[:, i * T:(i + 1) * T], x[:, :],
                                         b.tmp0[:, :], ALU.add), 1)

    scope.close()
    dump(b, "mix" + P, dst[:, :])


def ph_decay(b):
    nc, block, sm, cst = b.nc, b.block, b.sm, b.cst
    ph_mix(b, 0, "w", b.mixA)
    s0 = snap(b)
    scope = ExitStack()
    pw = psum(b, scope, "pw", [64, T])
    pd = [psum(b, scope, "pd0", [128, T]), psum(b, scope, "pd1", [128, T])]

    @block.tensor
    def _(t):
        t.wait_ge(sm["ve"].h, s0["ve"])
        for k in range(KC):
            lw = b.dw1s[:, k * 64:(k + 1) * 64]
            for h in range(2):
                mm = t.matmul(pw[:, h * 512:(h + 1) * 512], lw,
                              b.mixA[:, k * T + h * 512: k * T + (h + 1) * 512],
                              start=(k == 0), stop=(k == KC - 1))
        sm["te"].inc(mm, 1)
        t.wait_ge(sm["se"].h, s0["se"] + 1)
        for i in range(KC):
            if i >= 2:
                t.wait_ge(sm["se"].h, s0["se"] + i)
            lw = b.dw2s[:, i * 128:(i + 1) * 128]
            for h in range(2):
                mm = t.matmul(pd[i % 2][:, h * 512:(h + 1) * 512], lw,
                              b.tanhw[:, h * 512:(h + 1) * 512], start=True, stop=True)
            sm["te"].inc(mm, 1)

    @block.scalar
    def _(s):
        s.wait_ge(sm["te"].h, s0["te"] + 1)
        sm["se"].inc(s.activation(b.tanhw[:, :], pw[:, :], AF.Tanh), 1)
        for i in range(KC):
            s.wait_ge(sm["te"].h, s0["te"] + 1 + i + 1)
            if i >= 1:
                s.wait_ge(sm["ve"].h, s0["ve"] + 2 * i - 1)
            sm["se"].inc(s.activation(b.tmp0[:, :], pd[i % 2][:, :], AF.Exp,
                                      bias=cst["td"][:, i:i + 1]), 1)

    @block.vector
    def _(v):
        for i in range(KC):
            v.wait_ge(sm["se"].h, s0["se"] + 1 + i + 1)
            cch = cbch(b, i)
            for n in range(NCH):
                sc_i = v.tensor_tensor_scan(
                    cch[:, n * L:(n + 1) * L], b.tmp0[:, n * L:(n + 1) * L],
                    b.tmp0[:, n * L:(n + 1) * L], 0.0, ALU.add, ALU.bypass)
            sm["ve"].inc(sc_i, 1)
            v.drain()
            st = b.cstats
            base = i * (NCH * 6)
            c3 = cch.rearrange("p (n l) -> p n l", l=L)
            v.tensor_copy(st[:, base + 0 * NCH: base + 1 * NCH], c3[:, :, L // 2 - 1:L // 2])
            sm["ve"].inc(v.tensor_copy(st[:, base + 1 * NCH: base + 2 * NCH],
                                       c3[:, :, L - 1:L]), 1)

    s1 = snap(b)

    @block.vector
    def _(v):
        for i in range(KC):
            base = i * (NCH * 6)
            st = b.cstats
            if i >= 1:
                v.wait_ge(sm["se"].h, s1["se"] + i)
            v.tensor_tensor(b.tmp1[:, 0:NCH], st[:, base:base + NCH],
                            st[:, base + NCH: base + 2 * NCH], ALU.subtract)
            v.tensor_tensor_scan(b.tmp1[:, NCH:2 * NCH], st[:, base + NCH: base + 2 * NCH],
                                 st[:, base + NCH: base + 2 * NCH], 0.0, ALU.add, ALU.bypass)
            v.drain()
            v.tensor_copy(b.ctot[:, i:i + 1], b.tmp1[:, 2 * NCH - 1: 2 * NCH])
            v.tensor_tensor(b.tmp1[:, 3 * NCH:4 * NCH], b.tmp1[:, NCH:2 * NCH],
                            st[:, base + NCH:base + 2 * NCH], ALU.subtract)
            v.drain()
            sm["ve"].inc(v.tensor_tensor(b.tmp1[:, 4 * NCH:5 * NCH],
                                         b.tmp1[:, 3 * NCH:4 * NCH],
                                         st[:, base:base + NCH], ALU.add), 1)

    @block.scalar
    def _(s):
        for i in range(KC):
            base = i * (NCH * 6)
            st = b.cstats
            s.wait_ge(sm["ve"].h, s1["ve"] + i + 1)
            s.activation(st[:, base + 2 * NCH: base + 3 * NCH], b.tmp1[:, 0:NCH], AF.Exp)
            s.activation(st[:, base + 3 * NCH: base + 4 * NCH],
                         st[:, base + NCH: base + 2 * NCH], AF.Exp, scale=-1.0)
            s.activation(st[:, base + 4 * NCH: base + 5 * NCH],
                         st[:, base + 0 * NCH: base + 1 * NCH], AF.Exp, scale=-1.0)
            sm["se"].inc(s.activation(st[:, base + 5 * NCH: base + 6 * NCH],
                                      b.tmp1[:, 4 * NCH:5 * NCH], AF.Exp, scale=-1.0), 1)

    scope.close()
    dump(b, "cb_a", b.cb_a[:, :])
    dump(b, "cb_b", b.cb_b[:, :])
    dump(b, "cstats", b.cstats[:, :])
    dump(b, "ctot", b.ctot[:, :])


def ktc(b, i):
    if i < 8:
        return b.kt_a[:, i * T:(i + 1) * T]
    return b.kt_b[:, (i - 8) * T:(i - 8 + 1) * T]


def ydc(b, i):
    if i < 8:
        return b.yd_a[:, i * T:(i + 1) * T]
    return b.yd_b[:, (i - 8) * T:(i - 8 + 1) * T]


def cbch(b, o):
    if o < 8:
        return b.cb_a[:, o * T:(o + 1) * T]
    return b.cb_b[:, (o - 8) * T:(o - 8 + 1) * T]


def ph_proj_fm(b, P, dst_f, incl, orange=(0, KC)):
    nc, block, sm, cst = b.nc, b.block, b.sm, b.cst
    s0 = snap(b)
    dws = [sm["dwa"], sm["dwb"]]
    nv = 4 if P == "k" else 2
    ns = 2 if P == "k" else 1
    olist = list(range(*orange))
    scope = ExitStack()
    sfx = f"{P}{orange[0]}"
    pp = [psum(b, scope, f"pp{sfx}0", [128, T]), psum(b, scope, f"pp{sfx}1", [128, T])]
    wbuf = [scope.enter_context(nc.sbuf_tensor(f"wb{sfx}0", [128, C], BF16)),
            scope.enter_context(nc.sbuf_tensor(f"wb{sfx}1", [128, C], BF16))]

    @block.gpsimd
    def _(g):
        entry_waits(g, b, s0)
        for oi, o in enumerate(olist):
            if oi >= 2:
                g.wait_ge(sm["te"].h, s0["te"] + (oi - 1))
            src = b.di[f"W{P}T"][:, o * 128:(o + 1) * 128].rearrange("(k p) o -> p k o", p=128)
            dws[oi % 2].inc(g.dma_start(
                out=wbuf[oi % 2][:, :].rearrange("p (k o) -> p k o", o=128), in_=src), 16)

    @block.tensor
    def _(t):
        t.wait_ge(sm["ve"].h, s0["ve"])
        for oi, o in enumerate(olist):
            t.wait_ge(dws[oi % 2].h, s0["dwa" if oi % 2 == 0 else "dwb"] + 16 * (oi // 2 + 1))
            if oi >= 2:
                t.wait_ge(sm["ve"].h, s0["ve"] + nv * (oi - 1))
            for k in range(KC):
                lw = wbuf[oi % 2][:, k * 128:(k + 1) * 128]
                for h in range(2):
                    mm = t.matmul(pp[o % 2][:, h * 512:(h + 1) * 512], lw,
                                  b.mixA[:, k * T + h * 512: k * T + (h + 1) * 512],
                                  start=(k == 0), stop=(k == KC - 1))
            sm["te"].inc(mm, 1)

    def arg_incl(v, o):
        base = o * (NCH * 6)
        cch = cbch(b, o)
        cmid = b.cstats[:, base: base + NCH]
        for n in range(NCH):
            cm = cmid[:, n:n + 1]
            last = v.tensor_scalar(b.tmp1[:, n * L:(n + 1) * L], cch[:, n * L:(n + 1) * L],
                                   1.0, cm, ALU.mult, ALU.subtract)
        return last

    def arg_excl(v, o):
        base = o * (NCH * 6)
        cch = cbch(b, o)
        cmid = b.cstats[:, base: base + NCH]
        for n in range(NCH):
            cm = cmid[:, n:n + 1]
            v.tensor_scalar(b.tmp1[:, n * L: n * L + 1], cm, 1.0, None, ALU.mult)
            last = v.tensor_scalar(b.tmp1[:, n * L + 1:(n + 1) * L],
                                   cch[:, n * L:(n + 1) * L - 1], -1.0, cm, ALU.mult, ALU.add)
        return last

    @block.vector
    def _(v):
        for oi, o in enumerate(olist):
            dst = dst_f(o)
            if incl:
                sm["ve"].inc(arg_incl(v, o), 1)
            else:
                sm["ve"].inc(arg_excl(v, o), 1)
            v.wait_ge(sm["se"].h, s0["se"] + ns * oi + 1)
            v.wait_ge(sm["te"].h, s0["te"] + oi + 1)
            sm["ve"].inc(v.tensor_tensor(dst, pp[oi % 2][:, :], b.tmp2[:, :], ALU.mult), 1)
            if P == "k":
                sm["ve"].inc(arg_excl(v, o), 1)
                v.wait_ge(sm["se"].h, s0["se"] + ns * oi + 2)
                sm["ve"].inc(v.scalar_tensor_tensor(
                    ydc(b, o), pp[oi % 2][:, :],
                    cst["u"][:, o:o + 1], b.tmp2[:, :], ALU.mult, ALU.mult), 1)

    @block.scalar
    def _(s):
        for oi, o in enumerate(olist):
            s.wait_ge(sm["ve"].h, s0["ve"] + nv * oi + 1)
            sm["se"].inc(s.activation(b.tmp2[:, :], b.tmp1[:, :], AF.Exp), 1)
            if P == "k":
                s.wait_ge(sm["ve"].h, s0["ve"] + nv * oi + 3)
                sm["se"].inc(s.activation(b.tmp2[:, :], b.tmp1[:, :], AF.Exp, scale=-1.0), 1)

    scope.close()


def wq_chunks():
    out = []
    for i in range(KC):
        out.append(("gate_slice", i * 128, 0, 0))
    for i in range(KC):
        out.append(("val_slice", i * 128, 0, 1))
    for j in range(KC):
        out.append(("out_slice", 128 * (j // 2), 1024 * (j % 2), 2))
    return out


def ph_wq_sums(b):
    nc, block, sm, cst = b.nc, b.block, b.sm, b.cst
    s0 = snap(b)
    scope = ExitStack()
    pg = psum(b, scope, "pgam", [4, 1])
    wqst = [scope.enter_context(nc.sbuf_tensor("wqs0", [128, T], F32)),
            scope.enter_context(nc.sbuf_tensor("wqs1", [128, T], F32))]
    b.wqst = wqst
    chunks = wq_chunks()

    @block.sync
    def _(sy):
        sy.wait_ge(sm["dc"].h, s0["dc"])
        entry_waits(sy, b, s0)
        for idx, (nm, r0, c0, tag) in enumerate(chunks):
            if idx >= 2:
                sy.wait_ge(sm["ve"].h, s0["ve"] + (idx - 1))
            sm["dq"].inc(sy.dma_start(out=wqst[idx % 2][:, :],
                                      in_=b.di[nm][r0:r0 + 128, c0:c0 + 1024]), 16)

    @block.vector
    def _(v):
        for idx in range(48):
            v.wait_ge(sm["dq"].h, s0["dq"] + 16 * (idx + 1))
            sm["ve"].inc(v.tensor_reduce(b.tmp1[:, 40 + idx:41 + idx],
                                         wqst[idx % 2][:, :],
                                         AX.X, ALU.add, apply_absolute_value=True), 1)
        v.drain()
        v.tensor_reduce(b.tmp0[:, 0:1], b.tmp1[:, 40:56], AX.X, ALU.add)
        v.tensor_reduce(b.tmp0[:, 1:2], b.tmp1[:, 56:72], AX.X, ALU.add)
        v.tensor_reduce(b.tmp0[:, 2:3], b.tmp1[:, 72:88], AX.X, ALU.add)
        sm["ve"].inc(v.memset(b.tmp0[:, 3:4], 0.0), 1)

    @block.tensor
    def _(t):
        t.wait_ge(sm["ve"].h, s0["ve"] + 49)
        sm["te"].inc(t.matmul(pg[:, :], b.tmp0[:, 0:4], cst["ones"][:, 0:1],
                              start=True, stop=True), 1)

    @block.vector
    def _(v):
        v.wait_ge(sm["te"].h, s0["te"] + 1)
        sm["ve"].inc(v.tensor_copy(b.tmp0[0:4, 10:11], pg[:, :]), 1)

    @block.gpsimd
    def _(g):
        g.wait_ge(sm["ve"].h, s0["ve"] + 50)
        sm["dga"].inc(g.dma_start(out=b.cc_gam_in.ap().rearrange("o f -> f o"),
                                  in_=b.tmp0[0:4, 10:11]), 16)
        g.wait_ge(sm["dga"].h, sm["dga"].n)
        sm["cc"].inc(g.collective_compute(
            "AllReduce", ALU.add, replica_groups=[list(range(NCORES))],
            ins=[b.cc_gam_in.ap().opt()], outs=[b.cc_gam_out.ap().opt()]), 1)

    scope.close()


def ph_wq_ternary(b):
    nc, block, sm, cst = b.nc, b.block, b.sm, b.cst
    s0 = snap(b)
    scope = ExitStack()
    pb = psum(b, scope, "pthr", [128, 4])
    wqst = [scope.enter_context(nc.sbuf_tensor("wqt0", [128, T], F32)),
            scope.enter_context(nc.sbuf_tensor("wqt1", [128, T], F32))]
    wqo = [scope.enter_context(nc.sbuf_tensor("wqq0", [128, T], BF16)),
           scope.enter_context(nc.sbuf_tensor("wqq1", [128, T], BF16))]
    NELEM = float(C * HID)
    chunks = wq_chunks()

    @block.sync
    def _(sy):
        sy.wait_ge(sm["cc"].h, s0["cc"])
        sm["dq"].inc(sy.dma_start(out=b.gamsb[:, :], in_=b.cc_gam_out.ap()), 16)

    @block.tensor
    def _(t):
        t.wait_ge(sm["dq"].h, s0["dq"] + 16)
        sm["te"].inc(t.matmul(pb[:, :], cst["ones"][0:1, 0:128], b.gamsb[0:1, :],
                              start=True, stop=True), 1)

    @block.vector
    def _(v):
        v.wait_ge(sm["te"].h, s0["te"] + 1)
        v.memset(b.thr[:, :], 0.0)
        v.memset(b.thrn[:, :], 0.0)
        for mi in range(3):
            v.tensor_scalar(b.thr[:, mi:mi + 1], pb[:, mi:mi + 1],
                            1.0 / NELEM, 1e-5, ALU.mult, ALU.max)
        v.drain()
        v.tensor_scalar(b.thr[:, 4:7], b.thr[:, 0:3], 0.5, None, ALU.mult)
        v.drain()
        sm["ve"].inc(v.tensor_scalar(b.thrn[:, 4:7], b.thr[:, 4:7], -1.0, None, ALU.mult), 1)

    s1 = snap(b)

    @block.sync
    def _(sy):
        entry_waits(sy, b, s1)
        for idx, (nm, r0, c0, tag) in enumerate(chunks):
            if idx >= 2:
                sy.wait_ge(sm["ve"].h, s1["ve"] + 2 * (idx - 1))
            sm["dq"].inc(sy.dma_start(out=wqst[idx % 2][:, :],
                                      in_=b.di[nm][r0:r0 + 128, c0:c0 + 1024]), 16)

    @block.vector
    def _(v):
        v.wait_ge(sm["ve"].h, s1["ve"])
        for idx, (nm, r0, c0, tag) in enumerate(chunks):
            v.wait_ge(sm["dq"].h, s1["dq"] + 16 * (idx + 1))
            if idx >= 2:
                v.wait_ge(sm["dgb"].h, s1["dgb"] + 16 * nstores(idx - 2))
            st = wqst[idx % 2][:, :]
            v.tensor_scalar(b.tmp1[:, 0:T], st, b.thrn[:, 4 + tag:5 + tag], None, ALU.is_lt)
            sm["ve"].inc(v.scalar_tensor_tensor(
                wqo[idx % 2][:, :], st, b.thr[:, 4 + tag:5 + tag],
                b.tmp1[:, 0:T], ALU.is_gt, ALU.subtract), 2)

    @block.gpsimd
    def _(g):
        wq = b.cc_wq_in.ap()
        for idx, (nm, r0, c0, tag) in enumerate(chunks):
            g.wait_ge(sm["ve"].h, s1["ve"] + 2 * (idx + 1))
            qv = wqo[idx % 2][:, :]
            if tag < 2:
                i = r0 // 128
                for ml in range(8):
                    off = tag * (1024 * C) + (ml * KC + i) * 16384
                    dst = wq[0:1, off:off + 16384].rearrange("o (p j) -> (o p) j", p=128)
                    sm["dgb"].inc(g.dma_start(out=dst, in_=qv[:, ml * 128:(ml + 1) * 128]), 16)
            else:
                kl = r0 // 128
                off = 2 * (1024 * C) + kl * (128 * C)
                dst = wq[0:1, off:off + 128 * C].rearrange("o (p j) -> (o p) j", p=128)[:, c0:c0 + 1024]
                sm["dgb"].inc(g.dma_start(out=dst, in_=qv), 16)
        g.wait_ge(sm["dgb"].h, sm["dgb"].n)
        sm["cc"].inc(g.collective_compute(
            "AllGather", ALU.bypass, replica_groups=[list(range(NCORES))],
            ins=[b.cc_wq_in.ap().opt()], outs=[b.cc_wq_out.ap().opt()]), 1)

    dump(b, "thr", b.thr[:, :])
    scope.close()


def nstores(upto_idx):
    n = 0
    for idx, (nm, r0, c0, tag) in enumerate(wq_chunks()):
        if idx > upto_idx:
            break
        n += 8 if tag < 2 else 1
    return n


def ph_vproj(b):
    nc, block, sm = b.nc, b.block, b.sm
    s0 = snap(b)
    dws = [sm["dwa"], sm["dwb"]]
    scope = ExitStack()
    pv = [psum(b, scope, f"pv{j}", [128, 512]) for j in range(8)]
    wbuf = [scope.enter_context(nc.sbuf_tensor("wbv0", [128, C], BF16)),
            scope.enter_context(nc.sbuf_tensor("wbv1", [128, C], BF16))]

    @block.gpsimd
    def _(g):
        entry_waits(g, b, s0)
        for tg in range(4):
            for k in range(KC):
                gi = tg * KC + k
                if gi >= 2:
                    g.wait_ge(sm["te"].h, s0["te"] + gi - 1)
                src = b.di["WvT"][k * 128:(k + 1) * 128, :]
                dws[gi % 2].inc(g.dma_start(out=wbuf[gi % 2][:, :], in_=src), 16)

    @block.tensor
    def _(t):
        t.wait_ge(sm["ve"].h, s0["ve"])
        for tg in range(4):
            if tg >= 1:
                t.wait_ge(sm["ve"].h, s0["ve"] + tg)
            for k in range(KC):
                gi = tg * KC + k
                t.wait_ge(dws[gi % 2].h,
                          s0["dwa" if gi % 2 == 0 else "dwb"] + 16 * (gi // 2 + 1))
                for tt in range(2):
                    ttg = tg * 2 + tt
                    lhs = b.mixA[:, k * T + ttg * 128: k * T + (ttg + 1) * 128]
                    for o in range(4):
                        mm = t.matmul(pv[tt * 4 + o][:, :], lhs,
                                      wbuf[gi % 2][:, o * 512:(o + 1) * 512],
                                      start=(k == 0), stop=(k == KC - 1))
                sm["te"].inc(mm, 1)

    @block.vector
    def _(v):
        for tg in range(4):
            v.wait_ge(sm["te"].h, s0["te"] + (tg + 1) * KC)
            for tt in range(2):
                ttg = tg * 2 + tt
                for o in range(4):
                    last = (tt == 1 and o == 3)
                    ins = v.tensor_copy(b.vt[:, ttg * C + o * 512: ttg * C + (o + 1) * 512],
                                        pv[tt * 4 + o][:, :])
                    if last:
                        sm["ve"].inc(ins, 1)

    scope.close()



def ph_wkv(b):
    nc, block, sm, cst = b.nc, b.block, b.sm, b.cst
    s0 = snap(b)
    scope = ExitStack()
    pAU = [psum(b, scope, f"pAU{j}", [128, 512]) for j in range(2)]
    pYS = [psum(b, scope, f"pYS{j}", [128, 512]) for j in range(2)]
    pTt = [psum(b, scope, f"pTt{j}", [128, 2 * L], BF16) for j in range(2)]
    pA = [pAU[0][:, 0:128], pAU[1][:, 0:128]]
    pU = [pAU[0][:, 128:256], pAU[1][:, 128:256]]
    pY = [pYS[0][:, 0:64], pYS[1][:, 0:64]]
    pS = [pYS[0][:, 64:128], pYS[1][:, 64:128]]
    pY2 = [pYS[0][:, 128:192], pYS[1][:, 128:192]]
    pT = pTt

    @block.vector
    def _(v):
        v.wait_ge(sm["se"].h, s0["se"])
        v.wait_ge(sm["ve"].h, s0["ve"])
        sm["ve"].inc(v.memset(b.Spack[:, :], 0.0), 1)
        for e in range(WKV_EMAX if WKV_EMAX else NCH * KC):
            n, p = divmod(e, KC)
            rows = (n % 2) * 64
            base = p * (NCH * 6)
            ktile = ktc(b, p)[:, n * L:(n + 1) * L]
            par = e % 2
            v.tensor_scalar(b.s0s[:, :], b.Spack[:, p * L:(p + 1) * L],
                            b.cstats[:, base + 4 * NCH + n: base + 4 * NCH + n + 1],
                            None, ALU.mult)
            sm["ve"].inc(v.tensor_scalar(b.khf[:, :], ktile,
                                         b.cstats[:, base + 2 * NCH + n: base + 2 * NCH + n + 1],
                                         None, ALU.mult), 1)
            v.wait_ge(sm["te"].h, s0["te"] + 2 * e + 1)
            ms = cst["masksel"]
            v.tensor_tensor(b.tsel[rows:rows + 64, :], pA[par][rows:rows + 64, :],
                            ms[rows:rows + 64, 0:2 * L], ALU.mult)
            v.drain()
            v.select(b.asel[rows:rows + 64, :], cst["maskd8"][rows:rows + 64, :],
                     pU[par][rows:rows + 64, :], b.tsel[rows:rows + 64, :])
            sm["ve"].inc(v.tensor_copy(b.kht[rows:rows + 64, :], pT[par][rows:rows + 64, :]), 1)
            v.wait_ge(sm["te"].h, s0["te"] + 2 * e + 2)
            ytile = ydc(b, p)[:, n * L:(n + 1) * L]
            v.tensor_copy(ytile, pY[par][:, :])
            v.drain()
            v.tensor_tensor(ytile, ytile, pY2[par][:, :], ALU.add)
            sm["ve"].inc(v.scalar_tensor_tensor(
                b.Spack[:, p * L:(p + 1) * L], b.Spack[:, p * L:(p + 1) * L],
                b.cstats[:, base + 3 * NCH + n: base + 3 * NCH + n + 1],
                pS[par][:, :], ALU.mult, ALU.add), 1)

    @block.tensor
    def _(t):
        t.wait_ge(sm["ve"].h, s0["ve"] + 1)
        for e in range(WKV_EMAX if WKV_EMAX else NCH * KC):
            n, p = divmod(e, KC)
            rows = (n % 2) * 64
            par = e % 2
            tt = n // 2
            ktile = ktc(b, p)[:, n * L:(n + 1) * L]
            rtile = b.rt[:, p * T + n * L: p * T + (n + 1) * L]
            uktile = ydc(b, p)[:, n * L:(n + 1) * L]
            t.wait_ge(sm["ve"].h, s0["ve"] + 1 + 3 * e + 1)
            if e >= 2:
                t.wait_ge(sm["ve"].h, s0["ve"] + 1 + 3 * (e - 2) + 3)
            vt_ = lambda h: b.vt[rows:rows + 64,
                                 tt * C + (2 * p + h) * 64: tt * C + (2 * p + h + 1) * 64]
            mmA = lambda h: t.matmul(pA[par][rows:rows + 64, h * L:(h + 1) * L],
                                     ktile[h * 64:h * 64 + 64, :],
                                     rtile[h * 64:h * 64 + 64, :],
                                     start=True, stop=True, tile_position=(h * 64, rows))
            mmU = lambda h: t.matmul(pU[par][rows:rows + 64, h * L:(h + 1) * L],
                                     uktile[h * 64:h * 64 + 64, :],
                                     rtile[h * 64:h * 64 + 64, :],
                                     start=True, stop=True, tile_position=(h * 64, rows))
            mmY = lambda h: t.matmul(pY[par][h * 64:h * 64 + 64, :], vt_(h),
                                     b.asel[rows:rows + 64, h * L:(h + 1) * L],
                                     start=True, stop=True, tile_position=(rows, h * 64))
            mmS = lambda h: t.matmul(pS[par][h * 64:h * 64 + 64, :],
                                     b.kht[rows:rows + 64, h * L:(h + 1) * L],
                                     vt_(h), start=True, stop=True,
                                     tile_position=(rows, h * 64))
            mmY2 = lambda h: t.matmul(pY2[par][h * 64:h * 64 + 64, :],
                                      b.s0s[h * 64:h * 64 + 64, :],
                                      rtile[h * 64:h * 64 + 64, :],
                                      start=True, stop=True,
                                      tile_position=(h * 64, h * 64))

            def drain(mm):
                sm["tdr"].inc(mm, 1)
                t.wait_ge(sm["tdr"].h, sm["tdr"].n)

            mmA(0)
            mmU(0)
            drain(t.transpose(pT[par][rows:rows + 64, :], b.khf[:, :],
                              cst["ident"][:, :], tile_position=(0, rows)))
            mmA(1)
            sm["te"].inc(mmU(1), 1)
            if rows == 0:
                drain(mmY2(1))
                t.wait_ge(sm["ve"].h, s0["ve"] + 1 + 3 * e + 2)
                mmY(0); mmS(0); mmY(1); mmS(1)
                sm["te"].inc(mmY2(0), 1)
            else:
                t.wait_ge(sm["ve"].h, s0["ve"] + 1 + 3 * e + 2)
                mmY(0); mmS(0); mmY(1); mmS(1)
                drain(mmY2(1))
                sm["te"].inc(mmY2(0), 1)

    scope.close()
    dump(b, "Spack", b.Spack[:, :])
    dump(b, "yl_a", b.yd_a[:, :])
    dump(b, "yl_b", b.yd_b[:, :])


def ph_publish(b):
    nc, block, sm = b.nc, b.block, b.sm
    s0 = snap(b)
    scope = ExitStack()
    pt = psum(b, scope, "ptc", [16, 128])

    @block.tensor
    def _(t):
        t.wait_ge(sm["ve"].h, s0["ve"])
        sm["te"].inc(t.transpose(pt[:, :], b.ctot[:, :], cst_ident_f32(b)), 1)

    @block.vector
    def _(v):
        v.wait_ge(sm["te"].h, s0["te"] + 1)
        sm["ve"].inc(v.tensor_copy(b.ctrow[:, :], pt[:, :]), 1)

    @block.sync
    def _(sy):
        sy.wait_ge(sm["ve"].h, s0["ve"] + 1)
        dst = b.cc_s_in.ap()[0:1, 0:128 * T].rearrange("o (p t) -> (o p) t", p=128)
        sm["dq"].inc(sy.dma_start(out=dst, in_=b.Spack[:, :]), 16)
        dst = b.cc_s_in.ap()[0:1, 128 * T:].rearrange("o (n p) -> (o n) p", p=128)
        sm["dq"].inc(sy.dma_start(out=dst, in_=b.ctrow[:, :]), 16)

    @block.gpsimd
    def _(g):
        g.wait_ge(sm["dq"].h, s0["dq"] + 32)
        sm["cc"].inc(g.collective_compute(
            "AllGather", ALU.bypass, replica_groups=[list(range(NCORES))],
            ins=[b.cc_s_in.ap().opt()], outs=[b.cc_s_out.ap().opt()]), 1)

    scope.close()


def cst_ident_f32(b):
    return b.cst["identf"][:, :]


def ph_gproj(b):
    nc, block, sm = b.nc, b.block, b.sm
    s0 = snap(b)
    dws = [sm["dwa"], sm["dwb"]]
    scope = ExitStack()
    pp = [psum(b, scope, "ppg0", [128, T]), psum(b, scope, "ppg1", [128, T])]
    wbuf = [scope.enter_context(nc.sbuf_tensor("wbg0", [128, C], BF16)),
            scope.enter_context(nc.sbuf_tensor("wbg1", [128, C], BF16))]

    @block.gpsimd
    def _(g):
        entry_waits(g, b, s0)
        for o in range(KC):
            if o >= 2:
                g.wait_ge(sm["se"].h, s0["se"] + (o - 1))
            src = b.di["WgT"][:, o * 128:(o + 1) * 128].rearrange("(k p) o -> p k o", p=128)
            dws[o % 2].inc(g.dma_start(
                out=wbuf[o % 2][:, :].rearrange("p (k o) -> p k o", o=128), in_=src), 16)

    @block.tensor
    def _(t):
        entry_waits(t, b, s0)
        for o in range(KC):
            t.wait_ge(dws[o % 2].h, s0["dwa" if o % 2 == 0 else "dwb"] + 16 * (o // 2 + 1))
            if o >= 2:
                t.wait_ge(sm["se"].h, s0["se"] + (o - 1))
            for k in range(KC):
                lw = wbuf[o % 2][:, k * 128:(k + 1) * 128]
                for h in range(2):
                    mm = t.matmul(pp[o % 2][:, h * 512:(h + 1) * 512], lw,
                                  b.mixA[:, k * T + h * 512: k * T + (h + 1) * 512],
                                  start=(k == 0), stop=(k == KC - 1))
            sm["te"].inc(mm, 1)

    @block.scalar
    def _(s):
        for o in range(KC):
            s.wait_ge(sm["te"].h, s0["te"] + o + 1)
            sm["se"].inc(s.activation(ktc(b, o), pp[o % 2][:, :], AF.Silu), 1)

    scope.close()
    dump(b, "gb_a", b.kt_a[:, :])



def ph_combine_correct(b):
    nc, block, sm, cst = b.nc, b.block, b.sm, b.cst
    s0 = snap(b)
    scope = ExitStack()
    pL = psum(b, scope, "pL", [128, 8])
    pC = [psum(b, scope, f"pC{j}", [128, 512]) for j in range(2)]
    b.sinb = scope.enter_context(nc.sbuf_tensor("sinb", [128, T], BF16))
    b.w8 = scope.enter_context(nc.sbuf_tensor("w8", [128, KC * 8], F32))
    b.ctot9 = scope.enter_context(nc.sbuf_tensor("ctot9", [9, C], F32))
    b.sgst = [scope.enter_context(nc.sbuf_tensor("sg0", [128, T], F32)),
              scope.enter_context(nc.sbuf_tensor("sg1", [128, T], F32))]

    @block.sync
    def _(sy):
        sy.wait_ge(sm["cc"].h, s0["cc"])
        entry_waits(sy, b, s0)
        for e in range(NCORES):
            off = 128 * T
            src = b.cc_s_out.ap()[e:e + 1, off:off + C]
            sm["dq"].inc(sy.dma_start(out=b.ctot9[e:e + 1, :], in_=src), 16)

    @block.sync
    def _(sy):
        sm["dq"].inc(sy.dma_start(out=b.ctot9[8:9, :].rearrange("o (a c) -> o a c", a=4),
                                  in_=cst["ones"][0:4, 0:512]), 16)

    @block.vector
    def _(v):
        sm["ve"].inc(v.memset(b.vnop[:, :], 0.0), 1)

    @block.tensor
    def _(t):
        t.wait_ge(sm["dq"].h, s0["dq"] + 16 * (NCORES + 1))
        t.wait_ge(sm["ve"].h, s0["ve"] + 1)
        t.wait_ge(sm["se"].h, s0["se"])
        for i in range(KC):
            if i >= 1:
                t.wait_ge(sm["se"].h, s0["se"] + i)
            sm["te"].inc(t.matmul(pL[:, :], b.ctot9[:, i * 128:(i + 1) * 128],
                                  cst["m2aug"][:, :], start=True, stop=True), 1)

    @block.scalar
    def _(s):
        for i in range(KC):
            s.wait_ge(sm["te"].h, s0["te"] + i + 1)
            sm["se"].inc(s.activation(b.w8[:, i * 8:(i + 1) * 8], pL[:, :], AF.Exp,
                                      scale=-1.0), 1)

    s1 = snap(b)

    @block.sync
    def _(sy):
        entry_waits(sy, b, s1)
        for e in range(NCORES):
            if e >= 2:
                sy.wait_ge(sm["ve"].h, s1["ve"] + (e - 1))
            src = b.cc_s_out.ap()[e:e + 1, 0:128 * T].rearrange("o (p t) -> (o p) t", p=128)
            (sm["dxa"] if e % 2 == 0 else sm["dxb"]).inc(
                sy.dma_start(out=b.sgst[e % 2][:, :], in_=src), 16)

    @block.vector
    def _(v):
        v.wait_ge(sm["se"].h, s1["se"])
        for e in range(NCORES):
            v.wait_ge((sm["dxa"] if e % 2 == 0 else sm["dxb"]).h,
                      s1["dxa" if e % 2 == 0 else "dxb"] + 16 * (e // 2 + 1))
            for i in range(KC):
                w8c = b.w8[:, i * 8 + e: i * 8 + e + 1]
                seg = slice(i * L, (i + 1) * L)
                if e == 0:
                    v.tensor_scalar(b.tmp2[:, seg], b.sgst[0][:, seg], w8c, None, ALU.mult)
                else:
                    last = v.scalar_tensor_tensor(b.tmp2[:, seg], b.sgst[e % 2][:, seg],
                                                  w8c, b.tmp2[:, seg], ALU.mult, ALU.add)
            sm["ve"].inc(last if e > 0 else v.memset(b.vnop[:, :], 0.0), 1)
        sm["ve"].inc(v.tensor_copy(b.sinb[:, :], b.tmp2[:, :]), 1)
        for p in range(KC):
            base = p * (NCH * 6)
            for n in range(NCH):
                col = slice(p * T + n * L, p * T + (n + 1) * L)
                last = v.tensor_scalar(
                    b.mixA[:, col], b.rt[:, col],
                    b.cstats[:, base + 5 * NCH + n: base + 5 * NCH + n + 1],
                    None, ALU.mult)
            sm["ve"].inc(last, 1)

    s2 = snap(b)

    @block.tensor
    def _(t):
        t.wait_ge(sm["ve"].h, s2["ve"])
        for p in range(KC):
            for th in range(2):
                par = th
                if p > 0 or th > 0:
                    t.wait_ge(sm["ve"].h, s2["ve"] + 2 * p + th - 1)
                    t.wait_ge(sm["te"].h, s2["te"] + 2 * p + th)
                mm0 = t.matmul(pC[par][0:64, :],
                               b.sinb[0:64, p * L:(p + 1) * L],
                               b.mixA[0:64, p * T + th * 512: p * T + (th + 1) * 512],
                               start=True, stop=True, tile_position=(0, 0))
                sm["tdr"].inc(mm0, 1)
                t.wait_ge(sm["tdr"].h, sm["tdr"].n)
                mm = t.matmul(pC[par][64:128, :],
                              b.sinb[64:128, p * L:(p + 1) * L],
                              b.mixA[64:128, p * T + th * 512: p * T + (th + 1) * 512],
                              start=True, stop=True, tile_position=(64, 64))
                sm["te"].inc(mm, 1)

    @block.vector
    def _(v):
        for p in range(KC):
            for th in range(2):
                v.wait_ge(sm["te"].h, s2["te"] + 2 * p + th + 1)
                ych = ydc(b, p)[:, th * 512:(th + 1) * 512]
                sm["ve"].inc(v.tensor_tensor(ych, ych, pC[th][:, :], ALU.add), 1)

    scope.close()

    dump(b, "sin", b.sinb[:, :])
    dump(b, "ctot9", b.ctot9[:, :])
    dump(b, "w8", b.w8[:, :])
    dump(b, "yc_a", b.yd_a[:, :])
    dump(b, "yc_b", b.yd_b[:, :])
    dump(b, "w8", b.w8[:, :])


def ph_gn(b):
    nc, block, sm, cst = b.nc, b.block, b.sm, b.cst
    s0 = snap(b)
    scope = ExitStack()
    pS1 = psum(b, scope, "pgn1", [32, T])
    pS2 = psum(b, scope, "pgn2", [32, T])
    pA0 = [psum(b, scope, f"pa0{j}", [128, 512]) for j in range(2)]
    pB0 = [psum(b, scope, f"pb0{j}", [128, 512]) for j in range(2)]
    ysq = scope.enter_context(nc.sbuf_tensor("ysq", [128, KC * T], BF16))
    stat32 = scope.enter_context(nc.sbuf_tensor("stat32", [32, 3 * T], F32))
    statbf = scope.enter_context(nc.sbuf_tensor("statbf", [32, 2 * T], BF16))
    f_all = scope.enter_context(nc.sbuf_tensor("f_all_sb", [32, KC * 128], BF16))
    b.stat32 = stat32

    @block.sync
    def _(sy):
        entry_waits(sy, b, s0) if False else None
        for k in ("te", "ve", "se"):
            if b.sm[k].n:
                sy.wait_ge(b.sm[k].h, b.sm[k].n)
        sm["dc"].inc(sy.dma_start(out=f_all[:, :], in_=b.di["f_all"][:, :]), 16)

    @block.scalar
    def _(s):
        s.wait_ge(sm["ve"].h, s0["ve"])
        for i in range(KC):
            sm["se"].inc(s.activation(ysq[:, i * T:(i + 1) * T], ydc(b, i), AF.Square), 1)

    @block.tensor
    def _(t):
        t.wait_ge(sm["se"].h, s0["se"] + KC)
        for i in range(KC):
            E = cst["e_all"][:, i * 32:(i + 1) * 32]
            for th in range(2):
                t.matmul(pS1[:, th * 512:(th + 1) * 512], E,
                         ydc(b, i)[:, th * 512:(th + 1) * 512],
                         start=(i == 0), stop=(i == KC - 1))
                mm = t.matmul(pS2[:, th * 512:(th + 1) * 512], E,
                              ysq[:, i * T + th * 512: i * T + (th + 1) * 512],
                              start=(i == 0), stop=(i == KC - 1))
        sm["te"].inc(mm, 1)

    @block.vector
    def _(v):
        v.wait_ge(sm["te"].h, s0["te"] + 1)
        st = stat32
        v.tensor_scalar(st[:, 0:T], pS1[:, :], 1.0 / 64.0, None, ALU.mult)
        v.tensor_scalar(st[:, T:2 * T], pS2[:, :], 1.0 / 64.0, None, ALU.mult)
        v.tensor_tensor(b.tmp0[0:32, 0:T], st[:, 0:T], st[:, 0:T], ALU.mult)
        sm["ve"].inc(v.tensor_tensor(st[:, 2 * T:3 * T], st[:, T:2 * T],
                                     b.tmp0[0:32, 0:T], ALU.subtract), 1)

    @block.scalar
    def _(s):
        s.wait_ge(sm["ve"].h, s0["ve"] + 1)
        sm["se"].inc(s.activation(b.tmp0[0:32, 0:T], stat32[:, 2 * T:3 * T],
                                  AF.Sqrt, bias=cst["epsv"][0:32, 0:1]), 1)

    @block.vector
    def _(v):
        v.wait_ge(sm["se"].h, s0["se"] + KC + 1)
        v.reciprocal(stat32[:, 2 * T:3 * T], b.tmp0[0:32, 0:T])
        v.tensor_tensor(stat32[:, T:2 * T], stat32[:, 0:T],
                        stat32[:, 2 * T:3 * T], ALU.mult)
        v.tensor_copy(statbf[:, 0:T], stat32[:, 2 * T:3 * T])
        sm["ve"].inc(v.tensor_copy(statbf[:, T:2 * T], stat32[:, T:2 * T]), 1)

    s1 = snap(b)

    @block.tensor
    def _(t):
        t.wait_ge(sm["ve"].h, s1["ve"])
        t.wait_ge(sm["dc"].h, sm["dc"].n)
        for idx in range(2 * KC):
            i, th = divmod(idx, 2)
            par = idx % 2
            if idx >= 2:
                t.wait_ge(sm["ve"].h, s1["ve"] + (idx - 1))
            F = f_all[:, i * 128:(i + 1) * 128]
            t.matmul(pA0[par][:, :], F,
                     statbf[:, th * 512:(th + 1) * 512],
                     start=True, stop=True)
            mm = t.matmul(pB0[par][:, :], F,
                          statbf[:, T + th * 512: T + (th + 1) * 512],
                          start=True, stop=True)
            sm["te"].inc(mm, 1)

    @block.vector
    def _(v):
        for idx in range(2 * KC):
            i, th = divmod(idx, 2)
            par = idx % 2
            v.wait_ge(sm["te"].h, s1["te"] + idx + 1)
            ych = ydc(b, i)[:, th * 512:(th + 1) * 512]
            gch = ktc(b, i)[:, th * 512:(th + 1) * 512]
            v.tensor_tensor(b.tmp0[:, 0:512], ych, pA0[par][:, :], ALU.mult)
            v.tensor_tensor(b.tmp0[:, 0:512], b.tmp0[:, 0:512], pB0[par][:, :], ALU.subtract)
            v.tensor_scalar(b.tmp0[:, 0:512], b.tmp0[:, 0:512],
                            cst["lng"][:, i:i + 1], cst["lnb"][:, i:i + 1],
                            ALU.mult, ALU.add)
            sm["ve"].inc(v.tensor_tensor(ych, b.tmp0[:, 0:512], gch, ALU.mult), 1)

    scope.close()



def ph_wo_ln1(b):
    nc, block, sm, cst = b.nc, b.block, b.sm, b.cst
    s0 = snap(b)
    dws = [sm["dwa"], sm["dwb"]]
    scope = ExitStack()
    pW = [psum(b, scope, f"pw{j}", [128, 512]) for j in range(8)]
    stg = scope.enter_context(nc.sbuf_tensor("stg", [128, 8192], F32))
    b.q1tm = scope.enter_context(nc.sbuf_tensor("q1tm", [128, 2048], BF16))
    wbuf = [scope.enter_context(nc.sbuf_tensor("wbo0", [128, C], BF16)),
            scope.enter_context(nc.sbuf_tensor("wbo1", [128, C], BF16))]
    xtm0 = scope.enter_context(nc.sbuf_tensor("xtm0", [128, C], F32))
    xtm = [xtm0, xtm0]
    x1pre = stg[:, 0:2048]
    x1r = stg[:, 2048:4096]
    sqd = stg[:, 4096:6144]
    v1q = stg[:, 6144:8192]
    cvec = b.tmp1

    @block.gpsimd
    def _(g):
        entry_waits(g, b, s0)
        for tg in range(4):
            for k in range(KC):
                gi = tg * KC + k
                if gi >= 2:
                    g.wait_ge(sm["te"].h, s0["te"] + (gi - 1))
                src = b.di["WoT"][k * 128:(k + 1) * 128, :]
                dws[gi % 2].inc(g.dma_start(out=wbuf[gi % 2][:, :], in_=src), 16)

    @block.sync
    def _(sy):
        entry_waits(sy, b, s0)
        for tt in range(NTT):
            if tt >= 1:
                sy.wait_ge(sm["ve"].h, s0["ve"] + 4 * (tt - 1) + 1)
            src = b.di["x_tm"][tt * 128:(tt + 1) * 128, :]
            (sm["dxa"] if tt % 2 == 0 else sm["dxb"]).inc(
                sy.dma_start(out=xtm[tt % 2][:, :], in_=src), 16)
            if tt >= 1:
                sy.wait_ge(sm["ve"].h, s0["ve"] + 4 * (tt - 1) + 3)
                sm["dq"].inc(sy.dma_start(out=b.x1_scr[(tt - 1) * 128:tt * 128, :],
                                          in_=x1r), 16)
        sy.wait_ge(sm["ve"].h, s0["ve"] + 4 * (NTT - 1) + 3)
        sm["dq"].inc(sy.dma_start(out=b.x1_scr[(NTT - 1) * 128:NTT * 128, :],
                                  in_=x1r), 16)


    @block.tensor
    def _(t):
        t.wait_ge(sm["ve"].h, s0["ve"])
        for tg in range(4):
            if tg >= 1:
                t.wait_ge(sm["ve"].h, s0["ve"] + 4 * (2 * tg - 1) + 1)
            for k in range(KC):
                gi = tg * KC + k
                t.wait_ge(dws[gi % 2].h,
                          s0["dwa" if gi % 2 == 0 else "dwb"] + 16 * (gi // 2 + 1))
                for ttl in range(2):
                    tt = tg * 2 + ttl
                    lhs = ydc(b, k)[:, tt * 128:(tt + 1) * 128]
                    for o in range(4):
                        mm = t.matmul(pW[ttl * 4 + o][:, :], lhs,
                                      wbuf[gi % 2][:, o * 512:(o + 1) * 512],
                                      start=(k == 0), stop=(k == KC - 1))
                sm["te"].inc(mm, 1)

    @block.vector
    def _(v):
        for tt in range(NTT):
            tg, ttl = divmod(tt, 2)
            v.wait_ge(sm["te"].h, s0["te"] + 16 * (tg + 1))
            v.wait_ge((sm["dxa"] if tt % 2 == 0 else sm["dxb"]).h,
                      s0["dxa" if tt % 2 == 0 else "dxb"] + 16 * (tt // 2 + 1))
            if tt >= 1:
                v.wait_ge(sm["dq"].h, s0["dq"] + 16 * tt)
                v.wait_ge(sm["vq"].h, s0["vq"] + 256 * tt)
            for o in range(4):
                v.tensor_tensor(x1pre[:, o * 512:(o + 1) * 512], pW[ttl * 4 + o][:, :],
                                xtm[tt % 2][:, o * 512:(o + 1) * 512], ALU.add)
            sm["ve"].inc(v.tensor_reduce(cvec[:, 0:1], x1pre, AX.X, ALU.add), 1)
            v.wait_ge(sm["se"].h, s0["se"] + 2 * tt + 1)
            v.drain()
            v.tensor_scalar(cvec[:, 2:3], cvec[:, 0:1], 1.0 / 2048.0, None, ALU.mult)
            v.tensor_scalar(cvec[:, 3:4], cvec[:, 1:2], 1.0 / 2048.0, None, ALU.mult)
            v.drain()
            v.tensor_tensor(cvec[:, 4:5], cvec[:, 2:3], cvec[:, 2:3], ALU.mult)
            v.drain()
            sm["ve"].inc(v.tensor_tensor(cvec[:, 4:5], cvec[:, 3:4], cvec[:, 4:5],
                                         ALU.subtract), 1)
            v.wait_ge(sm["se"].h, s0["se"] + 2 * tt + 2)
            v.reciprocal(cvec[:, 6:7], cvec[:, 5:6])
            v.drain()
            v.tensor_scalar(x1r, x1pre, cvec[:, 2:3], cvec[:, 6:7], ALU.subtract, ALU.mult)
            v.tensor_reduce(cvec[:, 7:8], x1r, AX.X, ALU.max, apply_absolute_value=True)
            v.drain()
            v.tensor_scalar(cvec[:, 7:8], cvec[:, 7:8], 1e-5, None, ALU.max)
            v.drain()
            v.reciprocal(cvec[:, 9:10], cvec[:, 7:8])
            v.drain()
            v.tensor_scalar(cvec[:, 10:11], cvec[:, 9:10], 127.0, None, ALU.mult)
            sm["ve"].inc(v.tensor_scalar(b.sx1[:, tt:tt + 1], cvec[:, 7:8],
                                         1.0 / 127.0, None, ALU.mult), 1)
            v.drain()
            v.tensor_scalar(v1q, x1r, cvec[:, 10:11], 0.5, ALU.mult, ALU.add)
            fint = stg.bitcast(I32)[:, 0:2048]
            v.tensor_copy(fint, v1q)
            v.tensor_tensor(sqd, v1q, fint, ALU.is_lt)
            sm["ve"].inc(v.tensor_tensor(b.q1tm[:, :], fint, sqd, ALU.subtract), 1)

    @block.scalar
    def _(s):
        for tt in range(NTT):
            s.wait_ge(sm["ve"].h, s0["ve"] + 4 * tt + 1)
            sm["se"].inc(s.activation(sqd, x1pre, AF.Square, accum_out=cvec[:, 1:2]), 1)
            s.wait_ge(sm["ve"].h, s0["ve"] + 4 * tt + 2)
            sm["se"].inc(s.activation(cvec[:, 5:6], cvec[:, 4:5], AF.Sqrt,
                                      bias=cst["epsv"][:, 1:2]), 1)
            s.wait_ge(sm["ve"].h, s0["ve"] + 4 * tt + 4)
            for i in range(KC):
                sm["vq"].inc(s.dma_start(
                    out=b.mixA[:, i * T + tt * 128: i * T + (tt + 1) * 128],
                    in_=b.q1tm[:, i * 128:(i + 1) * 128], transpose=True), 16)

    scope.close()
    dump(b, "q1", b.mixA[:, :])
    dump(b, "sx1", b.sx1[:, :])


def ph_ffn(b):
    nc, block, sm, cst = b.nc, b.block, b.sm, b.cst
    ffs = ExitStack()
    FB = lambda name, shape, dt: ffs.enter_context(nc.sbuf_tensor(name, shape, dt))
    hbuf = ffs.enter_context(nc.sbuf_tensor("hbuf", [128, MC * T], BF16, side="right"))
    srow = FB("srow", [16, 128], F32)
    fibuf2 = FB("fibuf2", [128, 2048], I32)
    sbsc = ExitStack()
    sbc = sbsc.enter_context(nc.sbuf_tensor("sbc", [128, T], F32))
    gvs = ExitStack()
    gwb = [gvs.enter_context(nc.sbuf_tensor("gwb0", [128, 2048], BF16, side="right")),
           gvs.enter_context(nc.sbuf_tensor("gwb1", [128, 2048], BF16, side="right"))]
    vwb = [gvs.enter_context(nc.sbuf_tensor("vwb0", [128, 2048], BF16, side="right")),
           gvs.enter_context(nc.sbuf_tensor("vwb1", [128, 2048], BF16, side="right"))]
    mxf = b.tmp1.bitcast(BF16)[:, 0:1024]
    mnf = b.tmp1.bitcast(BF16)[:, 1024:2048]
    sbch = sbc
    mixAf = b.mixA.bitcast(F32)
    x2a = mixAf[:, 0:1024]
    x2b = mixAf[:, 1024:2048]
    sqb = mixAf[:, 2048:4096]
    s0 = snap(b)
    scope = ExitStack()
    scopeP1 = ExitStack()
    pB = [psum(b, scopeP1, f"pbc{j}", [128, 512]) for j in range(2)]
    pTs = psum(b, scopeP1, "pts", [8, 128])

    @block.tensor
    def _(t):
        entry_waits(t, b, s0)
        sm["te"].inc(t.transpose(pTs[:, :], b.sx1[:, :], cst["identf"][:, :]), 1)

    @block.vector
    def _(v):
        v.wait_ge(sm["te"].h, s0["te"] + 1)
        sm["ve"].inc(v.tensor_copy(srow[0:8, :], pTs[:, :]), 1)

    @block.gpsimd
    def _(g):
        g.wait_ge(sm["ve"].h, s0["ve"] + 1)
        sm["dga"].inc(g.dma_start(
            out=b.tmp0[0:1, 0:1024].rearrange("o (tt p) -> o tt p", p=128),
            in_=srow[0:8, :]), 16)

    @block.tensor
    def _(t):
        t.wait_ge(sm["dga"].h, s0["dga"] + 16)
        for th in range(2):
            mm = t.matmul(pB[th][:, :], cst["ones"][0:1, 0:128],
                          b.tmp0[0:1, th * 512:(th + 1) * 512], start=True, stop=True)
        sm["te"].inc(mm, 1)

    @block.vector
    def _(v):
        v.wait_ge(sm["te"].h, s0["te"] + 2)
        for th in range(2):
            cp = v.tensor_copy(sbc[:, th * 512:(th + 1) * 512], pB[th][:, :])
        sm["ve"].inc(cp, 1)

    scopeP1.close()
    scopeP2 = ExitStack()
    pg = [psum(b, scopeP2, f"pg{j}", [128, T]) for j in range(2)]
    pv = [psum(b, scopeP2, f"pvf{j}", [128, T]) for j in range(2)]
    s1 = snap(b)

    @block.sync
    def _(sy):
        sy.wait_ge(sm["cc"].h, s1["cc"])
        entry_waits(sy, b, s1)
        for m in range(MC):
            if m >= 2:
                sy.wait_ge(sm["te"].h, s1["te"] + (m - 1))
            csrc, ml = divmod(m, 8)
            off = (ml * KC) * 16384
            src = b.cc_wq_out.ap()[csrc:csrc + 1, off:off + 16 * 16384] \
                .rearrange("o (k p j) -> (o p) k j", p=128, j=128)
            (sm["dfa"] if m % 2 == 0 else sm["dfb"]).inc(
                sy.dma_start(out=gwb[m % 2][:, :].rearrange("p (k j) -> p k j", j=128),
                             in_=src), 16)
            off = 1024 * C + (ml * KC) * 16384
            src = b.cc_wq_out.ap()[csrc:csrc + 1, off:off + 16 * 16384] \
                .rearrange("o (k p j) -> (o p) k j", p=128, j=128)
            (sm["dfa"] if m % 2 == 0 else sm["dfb"]).inc(
                sy.dma_start(out=vwb[m % 2][:, :].rearrange("p (k j) -> p k j", j=128),
                             in_=src), 16)

    @block.tensor
    def _(t):
        t.wait_ge(sm["ve"].h, s1["ve"])
        for m in range(MC):
            par = m % 2
            dwk = "dfa" if par == 0 else "dfb"
            t.wait_ge(sm[dwk].h, s1[dwk] + 32 * (m // 2 + 1))
            if m >= 2:
                t.wait_ge(sm["ve"].h, s1["ve"] + 4 * (m - 1))
            for k in range(KC):
                for th in range(2):
                    t.matmul(pg[par][:, th * 512:(th + 1) * 512],
                             gwb[par][:, k * 128:(k + 1) * 128],
                             b.mixA[:, k * T + th * 512: k * T + (th + 1) * 512],
                             start=(k == 0), stop=(k == KC - 1))
                    mm = t.matmul(pv[par][:, th * 512:(th + 1) * 512],
                                  vwb[par][:, k * 128:(k + 1) * 128],
                                  b.mixA[:, k * T + th * 512: k * T + (th + 1) * 512],
                                  start=(k == 0), stop=(k == KC - 1))
            sm["te"].inc(mm, 1)

    @block.vector
    def _(v):
        for m in range(MC):
            par = m % 2
            v.wait_ge(sm["te"].h, s1["te"] + m + 1)
            for th in range(2):
                hsl = slice(m * T + th * 512, m * T + (th + 1) * 512)
                if m + th > 0:
                    v.wait_ge(sm["se"].h, s1["se"] + 2 * m + th)
                v.scalar_tensor_tensor(b.tmp0[:, 0:512], pg[par][:, th * 512:(th + 1) * 512],
                                       b.thr[:, 0:1], sbc[:, th * 512:(th + 1) * 512],
                                       ALU.mult, ALU.mult)
                sm["ve"].inc(v.scalar_tensor_tensor(
                    b.tmp0[:, 512:1024], pv[par][:, th * 512:(th + 1) * 512],
                    b.thr[:, 1:2], sbc[:, th * 512:(th + 1) * 512],
                    ALU.mult, ALU.mult), 1)
                v.wait_ge(sm["se"].h, s1["se"] + 2 * m + th + 1)
                v.scalar_tensor_tensor(hbuf[:, hsl], b.tmp0[:, 512:1024],
                                       cst["val_b"][:, m:m + 1], b.tmp2[:, 0:512],
                                       ALU.add, ALU.mult)
                if m == 0:
                    v.tensor_copy(mxf[:, th * 512:(th + 1) * 512], hbuf[:, hsl])
                    last = v.tensor_copy(mnf[:, th * 512:(th + 1) * 512], hbuf[:, hsl])
                else:
                    v.tensor_tensor(mxf[:, th * 512:(th + 1) * 512],
                                    mxf[:, th * 512:(th + 1) * 512], hbuf[:, hsl], ALU.max)
                    last = v.tensor_tensor(mnf[:, th * 512:(th + 1) * 512],
                                           mnf[:, th * 512:(th + 1) * 512], hbuf[:, hsl],
                                           ALU.min)
                sm["ve"].inc(last, 1)

    @block.scalar
    def _(s):
        for m in range(MC):
            for th in range(2):
                s.wait_ge(sm["ve"].h, s1["ve"] + 4 * m + 2 * th + 1)
                sm["se"].inc(s.activation(b.tmp2[:, 0:512], b.tmp0[:, 0:512], AF.Silu,
                                          bias=cst["gate_b"][:, m:m + 1]), 1)

    scopeP2.close()
    scopeP3 = ExitStack()
    pQh = [psum(b, scopeP3, f"pqh{j}", [128, 128], BF16) for j in range(2)]
    pTs2 = psum(b, scopeP3, "pts2", [8, 128])
    pB2 = [psum(b, scopeP3, f"pbc2{j}", [128, 512]) for j in range(2)]
    s2 = snap(b)

    @block.vector
    def _(v):
        v.tensor_scalar(b.tmp0[:, 0:1024], mnf[:, :], -1.0, None, ALU.mult)
        sm["ve"].inc(v.tensor_tensor(mxf[:, :], mxf[:, :], b.tmp0[:, 0:1024], ALU.max), 1)

    @block.tensor
    def _(t):
        t.wait_ge(sm["ve"].h, s2["ve"] + 1)
        for j in range(NTT):
            if j >= 2:
                t.wait_ge(sm["ve"].h, s2["ve"] + 1 + (j - 1))
            sm["te"].inc(t.transpose(pQh[j % 2][:, :], mxf[:, j * 128:(j + 1) * 128],
                                     cst["ident"][:, :]), 1)

    @block.vector
    def _(v):
        for j in range(NTT):
            v.wait_ge(sm["te"].h, s2["te"] + j + 1)
            sm["ve"].inc(v.tensor_reduce(b.tmp1[:, 20 + j:21 + j],
                                         pQh[j % 2][:, :], AX.X, ALU.max), 1)
        v.drain()
        v.tensor_scalar(b.tmp1[:, 20:28], b.tmp1[:, 20:28], 1e-5, None, ALU.max)
        v.drain()
        v.reciprocal(b.tmp1[:, 28:36], b.tmp1[:, 20:28])
        v.drain()
        v.tensor_scalar(b.tmp1[:, 36:44], b.tmp1[:, 28:36], 127.0, None, ALU.mult)
        v.tensor_scalar(b.tmp1[:, 44:52], b.tmp1[:, 20:28], 1.0 / 127.0, None, ALU.mult)
        v.drain()
        sm["ve"].inc(v.tensor_scalar(b.s3v[:, :], b.tmp1[:, 44:52], b.thr[:, 2:3],
                                     None, ALU.mult), 1)

    @block.tensor
    def _(t):
        t.wait_ge(sm["ve"].h, s2["ve"] + 1 + NTT + 1)
        sm["te"].inc(t.transpose(pTs2[:, :], b.tmp1[:, 36:44], cst["identf"][:, :]), 1)

    @block.vector
    def _(v):
        v.wait_ge(sm["te"].h, s2["te"] + NTT + 1)
        sm["ve"].inc(v.tensor_copy(srow[0:8, :], pTs2[:, :]), 1)

    @block.gpsimd
    def _(g):
        g.wait_ge(sm["ve"].h, s2["ve"] + NTT + 3)
        sm["dga"].inc(g.dma_start(
            out=b.tmp0[0:1, 0:1024].rearrange("o (tt p) -> o tt p", p=128),
            in_=srow[0:8, :]), 16)

    @block.tensor
    def _(t):
        t.wait_ge(sm["dga"].h, s2["dga"] + 16)
        for th in range(2):
            mm = t.matmul(pB2[th][:, :], cst["ones"][0:1, 0:128],
                          b.tmp0[0:1, th * 512:(th + 1) * 512], start=True, stop=True)
        sm["te"].inc(mm, 1)

    @block.vector
    def _(v):
        v.wait_ge(sm["te"].h, s2["te"] + NTT + 2)
        for th in range(2):
            cp = v.tensor_copy(sbch[:, th * 512:(th + 1) * 512], pB2[th][:, :])
        sm["ve"].inc(cp, 1)

    scopeP3.close()
    s3 = snap(b)

    @block.vector
    def _(v):
        v.wait_ge(sm["ve"].h, s3["ve"])
        for m in range(MC):
            hsl = slice(m * T, (m + 1) * T)
            v.tensor_tensor(b.tmp0[:, :], hbuf[:, hsl], sbch[:, :], ALU.mult)
            v.tensor_scalar(b.tmp1[:, :], b.tmp0[:, :], 0.5, None, ALU.add)
            v.tensor_copy(fibuf2[:, 0:1024], b.tmp1[:, :])
            v.tensor_tensor(b.tmp2[:, :], b.tmp1[:, :], fibuf2[:, 0:1024], ALU.is_lt)
            sm["ve"].inc(v.tensor_tensor(hbuf[:, hsl], fibuf2[:, 0:1024], b.tmp2[:, :],
                                         ALU.subtract), 1)

    gvs.close()
    sbsc.close()
    outsc = ExitStack()
    wob = [outsc.enter_context(nc.sbuf_tensor("wob0", [128, 2048], BF16, side="right")),
           outsc.enter_context(nc.sbuf_tensor("wob1", [128, 2048], BF16, side="right"))]
    x1st0 = outsc.enter_context(nc.sbuf_tensor("x1st0", [128, C], F32, side="right"))
    x1st = [x1st0, x1st0]
    out_b = mixAf[0:1, 6144:8192]
    s4 = snap(b)
    pO = [psum(b, scope, f"po{j}", [128, 512]) for j in range(8)]

    def store_out(sy, tt):
        sy.wait_ge(sm["ve"].h, s4["ve"] + 4 * tt + 3)
        sm["st"].inc(sy.dma_start(out=b.out_hbm[tt * 128:(tt + 1) * 128, 0:1024],
                                  in_=x2a), 16)
        sm["st"].inc(sy.dma_start(out=b.out_hbm[tt * 128:(tt + 1) * 128, 1024:2048],
                                  in_=x2b), 16)

    @block.sync
    def _(sy):
        sy.wait_ge(sm["te"].h, s4["te"])
        entry_waits(sy, b, s4)
        sm["dc"].inc(sy.dma_start(out=out_b.rearrange("o (a c) -> o a c", a=4),
                                  in_=b.di["out_b"][:, :]), 16)
        for tg in range(4):
            for ttl in range(2):
                tt = tg * 2 + ttl
                if tt >= 1:
                    sy.wait_ge(sm["ve"].h, s4["ve"] + 4 * (tt - 1) + 1)
                (sm["dxa"] if tt % 2 == 0 else sm["dxb"]).inc(
                    sy.dma_start(out=x1st[tt % 2][:, :],
                                 in_=b.x1_scr[tt * 128:(tt + 1) * 128, :]), 16)
                if tt >= 1:
                    store_out(sy, tt - 1)
        store_out(sy, NTT - 1)

    @block.gpsimd
    def _(g):
        entry_waits(g, b, s4)
        for tg in range(4):
            for k in range(MC):
                gi = tg * MC + k
                if gi >= 2:
                    g.wait_ge(sm["te"].h, s4["te"] + gi - 1)
                csrc, kl = divmod(k, 8)
                off = 2 * 1024 * C + kl * (128 * C)
                src = b.cc_wq_out.ap()[csrc:csrc + 1, off:off + 128 * C] \
                    .rearrange("o (p j) -> (o p) j", p=128)
                (sm["dga"] if gi % 2 == 0 else sm["dgb"]).inc(
                    g.dma_start(out=wob[gi % 2][:, :], in_=src), 16)

    @block.tensor
    def _(t):
        t.wait_ge(sm["dc"].h, sm["dc"].n)
        for tg in range(4):
            if tg >= 1:
                t.wait_ge(sm["ve"].h, s4["ve"] + 4 * (2 * tg - 1) + 1)
            for ttl in range(2):
                for o in range(4):
                    t.matmul(pO[ttl * 4 + o][:, :], cst["ones"][0:1, 0:128],
                             out_b[0:1, o * 512:(o + 1) * 512],
                             start=True, stop=False)
            for k in range(MC):
                gi = tg * MC + k
                dk = "dga" if gi % 2 == 0 else "dgb"
                t.wait_ge(sm[dk].h, s4[dk] + 16 * (gi // 2 + 1))
                for ttl in range(2):
                    tt = tg * 2 + ttl
                    lhs = hbuf[:, k * T + tt * 128: k * T + (tt + 1) * 128]
                    for o in range(4):
                        mm = t.matmul(pO[ttl * 4 + o][:, :], lhs,
                                      wob[gi % 2][:, o * 512:(o + 1) * 512],
                                      start=False, stop=(k == MC - 1))
                sm["te"].inc(mm, 1)

    @block.vector
    def _(v):
        for tt in range(NTT):
            tg, ttl = divmod(tt, 2)
            v.wait_ge(sm["te"].h, s4["te"] + 64 * (tg + 1))
            v.wait_ge((sm["dxa"] if tt % 2 == 0 else sm["dxb"]).h,
                      s4["dxa" if tt % 2 == 0 else "dxb"] + 16 * (tt // 2 + 1))
            if tt >= 1:
                v.wait_ge(sm["st"].h, s4["st"] + 32 * tt)
            for o in range(4):
                v.tensor_scalar(b.tmp2[:, 0:512], pO[ttl * 4 + o][:, :],
                                b.s3v[:, tt:tt + 1], None, ALU.mult)
                v.tensor_tensor(fibuf2.bitcast(F32)[:, o * 512:(o + 1) * 512],
                                b.tmp2[:, 0:512],
                                x1st[tt % 2][:, o * 512:(o + 1) * 512], ALU.add)
            x2pre = fibuf2.bitcast(F32)[:, 0:2048]
            sm["ve"].inc(v.tensor_reduce(b.tmp1[:, 0:1], x2pre, AX.X, ALU.add), 1)
            v.wait_ge(sm["se"].h, s4["se"] + 2 * tt + 1)
            v.drain()
            v.tensor_scalar(b.tmp1[:, 2:3], b.tmp1[:, 0:1], 1.0 / 2048.0, None, ALU.mult)
            v.tensor_scalar(b.tmp1[:, 3:4], b.tmp1[:, 1:2], 1.0 / 2048.0, None, ALU.mult)
            v.drain()
            v.tensor_tensor(b.tmp1[:, 4:5], b.tmp1[:, 2:3], b.tmp1[:, 2:3], ALU.mult)
            v.drain()
            sm["ve"].inc(v.tensor_tensor(b.tmp1[:, 4:5], b.tmp1[:, 3:4], b.tmp1[:, 4:5],
                                         ALU.subtract), 1)
            v.wait_ge(sm["se"].h, s4["se"] + 2 * tt + 2)
            v.reciprocal(b.tmp1[:, 6:7], b.tmp1[:, 5:6])
            v.drain()
            v.tensor_scalar(x2a, x2pre[:, 0:1024],
                            b.tmp1[:, 2:3], b.tmp1[:, 6:7], ALU.subtract, ALU.mult)
            sm["ve"].inc(v.tensor_scalar(x2b, x2pre[:, 1024:2048],
                                         b.tmp1[:, 2:3], b.tmp1[:, 6:7],
                                         ALU.subtract, ALU.mult), 2)

    @block.scalar
    def _(s):
        for tt in range(NTT):
            s.wait_ge(sm["ve"].h, s4["ve"] + 4 * tt + 1)
            x2pre = fibuf2.bitcast(F32)[:, 0:2048]
            sm["se"].inc(s.activation(sqb, x2pre, AF.Square,
                                      accum_out=b.tmp1[:, 1:2]), 1)
            s.wait_ge(sm["ve"].h, s4["ve"] + 4 * tt + 2)
            sm["se"].inc(s.activation(b.tmp1[:, 5:6], b.tmp1[:, 4:5], AF.Sqrt,
                                      bias=cst["epsv"][:, 1:2]), 1)

    scope.close()
    outsc.close()
    ffs.close()



def prep_host_constants():
    Ms = np.triu(np.ones((L, L), np.float32), 1)
    Md = np.eye(L, dtype=np.float32)
    masksel = np.concatenate([np.tile(Ms, (2, 2)), np.tile(Md, (2, 2))], axis=1)
    e_all = np.zeros((128, KC * 32), np.float32)
    f_all = np.zeros((32, KC * 128), np.float32)
    for i in range(KC):
        for p in range(128):
            h = 2 * i + p // 64
            e_all[p, i * 32 + h] = 1.0
            f_all[h, i * 128 + p] = 1.0
    import ml_dtypes
    bf = ml_dtypes.bfloat16
    return dict(
        ident_bf=np.eye(128, dtype=np.float32).astype(bf),
        ident_f32=np.eye(128, dtype=np.float32),
        masksel=masksel.astype(np.float32),
        maskdiag_u8=np.tile(np.eye(64, dtype=np.uint8), (2, 2)),
        e_all=e_all.astype(bf),
        f_all=f_all.astype(bf),
        ones_f32=np.ones((128, 512), np.float32),
        epsv=np.tile(np.array([[1e-5 * 64.0, 1e-5]], np.float32), (128, 1)),
    )


def vec_fm(v):
    return np.ascontiguousarray(np.asarray(v, np.float32).reshape(KC, 128).T)


def prep_inputs(ins, core):
    f32 = np.float32
    x = np.asarray(ins["L0_attn_x"], f32)[0]
    sl = slice(core * T, (core + 1) * T)
    x_sl = x[sl]
    xprev = np.zeros((C,), f32) if core == 0 else x[core * T - 1]
    m2aug = np.zeros((9, 8), f32)
    for bcore in range(8):
        if bcore < core:
            for e in range(bcore + 1, core):
                m2aug[e, bcore] = 1.0
        else:
            m2aug[8, bcore] = 1e9
    gate_wT = np.asarray(ins["L0_ffn_gate_w"], f32).T
    val_wT = np.asarray(ins["L0_ffn_val_w"], f32).T
    out_wT = np.asarray(ins["L0_ffn_out_w"], f32).T
    maas = np.zeros((128, KC * 6), f32)
    for j, nm in enumerate("xwkvrg"):
        v = vec_fm(ins[f"L0_attn_time_maa_{nm}"])
        for i in range(KC):
            maas[:, i * 6 + j] = v[:, i]
    d = dict(
        x_fm=np.ascontiguousarray(x_sl.T),
        xprev=vec_fm(xprev),
        x_tm=np.ascontiguousarray(x_sl),
        WrT=np.ascontiguousarray(np.asarray(ins["L0_attn_W_r"], f32).T),
        WkT=np.ascontiguousarray(np.asarray(ins["L0_attn_W_k"], f32).T),
        WvT=np.ascontiguousarray(np.asarray(ins["L0_attn_W_v"], f32).T),
        WgT=np.ascontiguousarray(np.asarray(ins["L0_attn_W_g"], f32).T),
        WoT=np.ascontiguousarray(np.asarray(ins["L0_attn_W_o"], f32).T),
        maa_w1=np.asarray(ins["L0_attn_time_maa_w1"], f32),
        maa_w2=np.asarray(ins["L0_attn_time_maa_w2"], f32),
        dw1=np.asarray(ins["L0_attn_time_decay_w1"], f32),
        dw2=np.asarray(ins["L0_attn_time_decay_w2"], f32),
        maa_vecs=maas,
        td_vec=vec_fm(ins["L0_attn_time_decay"]),
        u_vec=vec_fm(np.asarray(ins["L0_attn_time_faaaa"], f32).reshape(-1)),
        lng=vec_fm(ins["L0_attn_ln_gamma"]),
        lnb=vec_fm(ins["L0_attn_ln_beta"]),
        m2aug=m2aug,
        gate_slice=np.ascontiguousarray(gate_wT[:, core * 1024:(core + 1) * 1024]),
        val_slice=np.ascontiguousarray(val_wT[:, core * 1024:(core + 1) * 1024]),
        out_slice=np.ascontiguousarray(out_wT[core * 1024:(core + 1) * 1024, :]),
        gate_b=vec_fm_h(ins["L0_ffn_gate_b"]),
        val_b=vec_fm_h(ins["L0_ffn_val_b"]),
        out_b=np.asarray(ins["L0_ffn_out_b"], f32).reshape(4, 512),
    )
    d.update(prep_host_constants())
    return d


def vec_fm_h(v):
    return np.ascontiguousarray(np.asarray(v, np.float32).reshape(MC, 128).T)


def run(ins, stop_after=None, dumps=(), cores=NCORES):
    from concourse.bass_utils import run_bass_kernel_spmd
    nc, b = build(stop_after=stop_after, dumps=dumps)
    in_maps = [prep_inputs(ins, c) for c in range(cores)]
    res = run_bass_kernel_spmd(nc, in_maps, core_ids=list(range(cores)))
    return res, b


def kernel(**inputs):
    from concourse.bass_utils import run_bass_kernel_spmd
    nc, b = build()
    in_maps = [prep_inputs(inputs, c) for c in range(NCORES)]
    res = run_bass_kernel_spmd(nc, in_maps, core_ids=list(range(NCORES)))
    out = np.concatenate([res.results[c]["out"] for c in range(NCORES)], axis=0)
    return out.reshape(1, 8192, C)


# version 06 (final)
# speedup vs baseline: 1.0092x; 1.0092x over previous
import numpy as np
from contextlib import ExitStack

import concourse.bass as bass
import concourse.mybir as mybir

F32 = mybir.dt.float32
BF16 = mybir.dt.bfloat16
I32 = mybir.dt.int32
AF = mybir.ActivationFunctionType
ALU = mybir.AluOpType
AX = mybir.AxisListType

C = 2048
T = 1024
L = 64
NCH = T // L
KC = C // 128
HID = 8192
MC = HID // 128
NCORES = 8
GEPS = float(np.float32(1e-5 * 8.0 ** 2))
LEPS = 1e-5
NTT = T // 128
WQ = 3 * 1024 * C
WKV_EMAX = None


class Sem:
    def __init__(self, nc, es, name):
        self.h = es.enter_context(nc.semaphore(name))
        self.n = 0

    def inc(self, instr, k=1):
        instr.then_inc(self.h, k)
        self.n += k
        return self.n


def entry_waits(eng, b, s0):
    for k in ("te", "ve", "se"):
        if s0[k]:
            eng.wait_ge(b.sm[k].h, s0[k])


def snap(b):
    return {k: s.n for k, s in b.sm.items()}


def build(stop_after=None, dumps=()):
    nc = bass.Bass("TRN2", target_bir_lowering=False, debug=False,
                   detect_race_conditions=False)
    es = ExitStack()
    b = type("B", (), {})()
    b.nc, b.es, b.dumps = nc, es, set(dumps)
    b.extra_outs = []
    di = {}

    def dram_in(name, shape, dt=F32):
        di[name] = nc.dram_tensor(name, shape, dt, kind="ExternalInput").ap()

    b.di = di
    dram_in("x_fm", [C, T])
    dram_in("xprev", [128, KC])
    dram_in("x_tm", [T, C])
    for w in ("v", "o"):
        dram_in(f"W{w}T", [C, C])
    for w in ("r", "k", "g"):
        dram_in(f"W{w}L", [C, C])
    dram_in("maa_w1L", [128, KC * 160])
    dram_in("maa_w2", [5, 32, C])
    dram_in("dw1L", [128, KC * 64])
    dram_in("dw2", [64, C])
    dram_in("maa_vecs", [128, KC * 6])
    dram_in("td_vec", [128, KC])
    dram_in("u_vec", [128, KC])
    dram_in("lng", [128, KC])
    dram_in("lnb", [128, KC])
    dram_in("ident_bf", [128, 128], BF16)
    dram_in("ident_f32", [128, 128], F32)
    dram_in("masksel", [128, 4 * L])
    dram_in("maskdiag_u8", [128, 2 * L], mybir.dt.uint8)
    dram_in("e_all", [128, KC * 32], BF16)
    dram_in("f_all", [32, KC * 128], BF16)
    dram_in("ones_f32", [128, 512])
    dram_in("epsv", [128, 2])
    dram_in("m2aug", [9, 8])
    dram_in("gate_slice", [C, 1024])
    dram_in("val_slice", [C, 1024])
    dram_in("out_slice", [1024, C])
    dram_in("gate_b", [128, MC])
    dram_in("val_b", [128, MC])
    dram_in("out_b", [4, 512])

    b.out_hbm = nc.dram_tensor("out", [T, C], F32, kind="ExternalOutput").ap()
    b.x1_scr = nc.dram_tensor("x1_scr", [T, C], F32).ap()
    b.cc_s_in = nc.dram_tensor("cc_s_in", [1, 128 * T + C], F32)
    b.cc_s_out = nc.dram_tensor("cc_s_out", [NCORES, 128 * T + C], F32)
    b.cc_gam_in = nc.dram_tensor("cc_gam_in", [1, 4], F32)
    b.cc_gam_out = nc.dram_tensor("cc_gam_out", [1, 4], F32)
    b.cc_wq_in = nc.dram_tensor("cc_wq_in", [1, WQ], BF16)
    b.cc_wq_out = nc.dram_tensor("cc_wq_out", [NCORES, WQ], BF16)

    S = lambda name: Sem(nc, es, name)
    b.sm = dict(dxa=S("dxa"), dxb=S("dxb"), dwa=S("dwa"), dwb=S("dwb"), dc=S("dc"),
                dga=S("dga"), dgb=S("dgb"), dq=S("dq"), st=S("st"), te=S("te"),
                ve=S("ve"), se=S("se"), ge=S("ge"), cc=S("ccs"), vq=S("vq"),
                dfa=S("dfa"), dfb=S("dfb"), tdr=S("tdr"))

    b.sb = lambda name, shape, dt=F32: es.enter_context(nc.sbuf_tensor(name, shape, dt))
    b.block = es.enter_context(nc.Block())
    b._stacks = []
    emit(b, stop_after)
    for s in reversed(b._stacks):
        s.close()
    es.close()
    return nc, b


def psum(b, scope, name, shape, dt=F32):
    return scope.enter_context(b.nc.psum_tensor(name, shape, dt))


def emit(b, stop_after):
    nc, block, sm = b.nc, b.block, b.sm
    sb = b.sb

    cst = b.cst = {}
    pairs = [("maa", "maa_vecs", [128, KC * 6], F32), ("td", "td_vec", [128, KC], F32),
             ("u", "u_vec", [128, KC], F32), ("lng", "lng", [128, KC], F32),
             ("lnb", "lnb", [128, KC], F32), ("ident", "ident_bf", [128, 128], BF16),
             ("identf", "ident_f32", [128, 128], F32),
             ("masksel", "masksel", [128, 4 * L], F32),
             ("maskd8", "maskdiag_u8", [128, 2 * L], mybir.dt.uint8),
             ("e_all", "e_all", [128, KC * 32], BF16),
             ("ones", "ones_f32", [128, 512], F32), ("epsv", "epsv", [128, 2], F32),
             ("m2aug", "m2aug", [9, 8], F32),
             ("xprev", "xprev", [128, KC], F32),
             ("gate_b", "gate_b", [128, MC], F32), ("val_b", "val_b", [128, MC], F32)]
    for nm, src, shape, dt in pairs:
        cst[nm] = sb("c_" + nm, shape, dt)

    @block.sync
    def _(sy):
        for nm, src, shape, dt in pairs:
            sm["dc"].inc(sy.dma_start(out=cst[nm][:, :], in_=b.di[src][:, :]), 16)

    if stop_after == "consts":
        return fin(b)
    b.mixA = sb("mixA", [128, KC * T], BF16)
    b.ctot = sb("ctot", [128, KC])
    b.tmp0 = sb("tmp0", [128, T])
    b.tmp1 = sb("tmp1", [128, T])
    b.tmp2 = sb("tmp2", [128, T])
    b.gamsb = sb("gamsb", [1, 4])
    b.thr = sb("thr", [128, 8])
    b.thrn = sb("thrn", [128, 8])
    b.sx1 = sb("sx1", [128, NTT])
    b.s3v = sb("s3v", [128, NTT])
    b.junk = sb("junk", [128, 1])
    b.vnop = sb("vnop", [1, 2])
    b.csc = ExitStack()
    b._stacks.append(b.csc)
    b.cstats = b.csc.enter_context(nc.sbuf_tensor("cstats", [128, KC * NCH * 6], F32))
    b.xxt = b.csc.enter_context(nc.sbuf_tensor("xxt", [128, T], BF16))

    b.att = ExitStack()
    b._stacks.append(b.att)
    A = lambda name, shape, dt=F32: b.att.enter_context(nc.sbuf_tensor(name, shape, dt))
    b.early = ExitStack()
    b._stacks.append(b.early)
    E = lambda name, shape, dt=F32: b.early.enter_context(nc.sbuf_tensor(name, shape, dt))
    b.a_pack = E("a_pack", [128, T], BF16)
    b.a4 = E("a4", [32, T], BF16)
    b.w2s = E("w2s", [128, 2 * C], BF16)
    b.early2 = ExitStack()
    b._stacks.append(b.early2)
    E2 = lambda name, shape, dt=F32: b.early2.enter_context(
        nc.sbuf_tensor(name, shape, dt, side="right"))
    b.w1s = E2("w1s", [128, KC * 160], BF16)
    b.dw1s = E2("dw1s", [128, KC * 64], BF16)
    b.dw2s = E2("dw2s", [64, C], BF16)
    b.tanhw = E2("tanhw", [64, T], BF16)
    b.cscope_a = ExitStack()
    b.cscope_b = ExitStack()
    b._stacks.append(b.cscope_b)
    b._stacks.append(b.cscope_a)
    b.cb_b = b.cscope_b.enter_context(nc.sbuf_tensor("cbuf_b", [128, 8 * T], F32))
    b.cb_a = b.cscope_a.enter_context(nc.sbuf_tensor("cbuf_a", [128, 8 * T], F32))

    ph_x_xx_xxx(b)
    if stop_after == "xx":
        return fin(b)
    ph_a(b, sub=stop_after if stop_after in ("a1", "a2") else None)
    if stop_after in ("a", "a1", "a2"):
        return fin(b)
    ph_decay(b)
    b.early2.close()
    if stop_after == "decay":
        return fin(b)
    b.bsc = ExitStack()
    b._stacks.append(b.bsc)
    R = lambda name, w: b.bsc.enter_context(nc.sbuf_tensor(name, [128, w], BF16, side="right"))
    b.rt = R("rt", KC * T)
    ph_mix(b, 3, "r", b.mixA)
    ph_proj_fm(b, "r", lambda o: b.rt[:, o * T:(o + 1) * T], incl=False)
    ph_mix(b, 1, "k", b.mixA)
    b.kt_a = R("kt_a", 8 * T)
    b.yd_a = R("yd_a", 8 * T)
    ph_proj_fm(b, "k", lambda o: ktc(b, o), incl=True, orange=(0, 8))
    b.cscope_a.close()
    b.kt_b = R("kt_b", 8 * T)
    b.yd_b = R("yd_b", 8 * T)
    ph_proj_fm(b, "k", lambda o: ktc(b, o), incl=True, orange=(8, 16))
    b.cscope_b.close()
    dump(b, "uk_a", b.yd_a[:, :])
    dump(b, "uk_b", b.yd_b[:, :])
    dump(b, "kt_a", b.kt_a[:, :])
    dump(b, "rt", b.rt[:, :])
    if stop_after == "kr":
        return fin(b)
    ph_wq_sums(b)
    ph_wq_ternary(b)
    if stop_after == "wq":
        return fin(b)
    ph_mix(b, 2, "v", b.mixA)
    b.vsc = ExitStack()
    b._stacks.append(b.vsc)
    b.vt = b.vsc.enter_context(nc.sbuf_tensor("vt", [128, NTT * C], BF16, side="right"))
    ph_vproj(b)
    dump(b, "vt", b.vt[:, :])
    ph_mix(b, 4, "g", b.mixA)
    b.early.close()
    if stop_after == "prewkv":
        return fin(b)
    b.wsc = ExitStack()
    b._stacks.append(b.wsc)
    W = lambda name, shape, dt=F32: b.wsc.enter_context(nc.sbuf_tensor(name, shape, dt))
    b.Spack = W("Spack", [128, T], F32)
    b.s0s = W("s0s", [128, L], BF16)
    b.khf = W("khf", [128, L], BF16)
    b.kht = W("kht", [128, 2 * L], BF16)
    b.asel = W("asel", [128, 2 * L], BF16)
    b.tsel = W("tsel", [128, 2 * L], BF16)
    b.ctrow = W("ctrow", [16, 128], F32)
    ph_wkv(b)
    if stop_after == "wkv":
        return fin(b)
    ph_publish(b)
    ph_gproj(b)
    b.vsc.close()
    b.wsc.close()
    ph_combine_correct(b)
    b.csc.close()
    if stop_after == "corr":
        return fin(b)
    ph_gn(b)
    dump(b, "yng_a", b.yd_a[:, :])
    dump(b, "yng_b", b.yd_b[:, :])
    if stop_after == "gn":
        return fin(b)
    ph_wo_ln1(b)
    if stop_after == "x1":
        return fin(b)
    b.bsc.close()
    b.att.close()
    ph_ffn(b)
    fin(b)


def fin(b):
    block, sm = b.block, b.sm
    tgt = {k: s.n for k, s in sm.items()}

    @block.sync
    def _(sy):
        for k, v in tgt.items():
            if v:
                sy.wait_ge(sm[k].h, v)

    @block.gpsimd
    def _(g):
        g.wait_ge(sm["st"].h, sm["st"].n)


def dump(b, name, ap):
    if name not in b.dumps:
        return
    d = b.nc.dram_tensor("dump_" + name, list(ap.shape), ap.dtype, kind="ExternalOutput").ap()
    b.extra_outs.append("dump_" + name)
    sm = b.sm
    tgt = {k: s.n for k, s in sm.items()}

    @b.block.sync
    def _(sy):
        for k, v in tgt.items():
            if v:
                sy.wait_ge(sm[k].h, v)
        sm["st"].inc(sy.dma_start(out=d, in_=ap), 16)



def ph_x_xx_xxx(b):
    block, sm, cst = b.block, b.sm, b.cst
    s0 = snap(b)
    dxs = [sm["dxa"], sm["dxb"]]
    sc = ExitStack()
    xst = [psum(b, sc, "xs0", [128, T]) if False else
           sc.enter_context(b.nc.sbuf_tensor("xsA0", [128, T], F32)),
           sc.enter_context(b.nc.sbuf_tensor("xsA1", [128, T], F32))]

    @block.sync
    def _(sy):
        sy.wait_ge(sm["dc"].h, s0["dc"])
        for i in range(KC):
            if i >= 2:
                sy.wait_ge(sm["ve"].h, s0["ve"] + 3 * (i - 1))
            dxs[i % 2].inc(sy.dma_start(out=xst[i % 2][:, :],
                                        in_=b.di["x_fm"][i * 128:(i + 1) * 128, :]), 16)

    @block.vector
    def _(v):
        for i in range(KC):
            x = xst[i % 2]
            v.wait_ge(dxs[i % 2].h, s0["dxa" if i % 2 == 0 else "dxb"] + 16 * (i // 2 + 1))
            v.tensor_tensor(b.xxt[:, 0:1], cst["xprev"][:, i:i + 1], x[:, 0:1], ALU.subtract)
            v.tensor_tensor(b.xxt[:, 1:T], x[:, 0:T - 1], x[:, 1:T], ALU.subtract)
            sm["ve"].inc(v.scalar_tensor_tensor(
                b.mixA[:, i * T:(i + 1) * T], b.xxt[:, :], cst["maa"][:, i * 6:i * 6 + 1],
                x[:, :], ALU.mult, ALU.add), 3)

    sc.close()


def ph_a(b, sub=None):
    nc, block, sm = b.nc, b.block, b.sm
    s0 = snap(b)

    @block.gpsimd
    def _(g):
        sm["dga"].inc(g.dma_start(out=b.w1s[:, :], in_=b.di["maa_w1L"][:, :]), 16)
        sm["dga"].inc(g.dma_start(
            out=b.w2s[:, 0:C],
            in_=b.di["maa_w2"][0:4].rearrange("f d c -> (f d) c")), 16)
        sm["dga"].inc(g.dma_start(out=b.w2s[0:32, C:2 * C], in_=b.di["maa_w2"][4]), 16)
        sm["dga"].inc(g.dma_start(out=b.dw1s[:, :], in_=b.di["dw1L"][:, :]), 16)
        sm["dga"].inc(g.dma_start(out=b.dw2s[:, :], in_=b.di["dw2"][:, :]), 16)

    if sub == "a1":
        return

    scope = ExitStack()
    pa = psum(b, scope, "pa", [32, T])

    @block.tensor
    def _(t):
        t.wait_ge(sm["ve"].h, s0["ve"])
        t.wait_ge(sm["dga"].h, s0["dga"] + 16)
        for f in range(5):
            if f >= 1 and sub != "a2":
                t.wait_ge(sm["se"].h, s0["se"] + f)
            for k in range(KC):
                lw = b.w1s[:, k * 160 + f * 32: k * 160 + (f + 1) * 32]
                for h in range(2):
                    mm = t.matmul(pa[:, h * 512:(h + 1) * 512], lw,
                                  b.mixA[:, k * T + h * 512: k * T + (h + 1) * 512],
                                  start=(k == 0), stop=(k == KC - 1))
            sm["te"].inc(mm, 1)

    if sub == "a2":
        scope.close()
        return

    @block.scalar
    def _(s):
        for f in range(5):
            s.wait_ge(sm["te"].h, s0["te"] + f + 1)
            dst = b.a_pack[32 * f:32 * (f + 1), :] if f < 4 else b.a4[:, :]
            sm["se"].inc(s.activation(dst, pa[:, :], AF.Tanh), 1)

    scope.close()
    dump(b, "a_pack", b.a_pack[:, :])


def ph_mix(b, fidx, P, dst):
    nc, block, sm, cst = b.nc, b.block, b.sm, b.cst
    s0 = snap(b)
    midx = {"x": 0, "w": 1, "k": 2, "v": 3, "r": 4, "g": 5}[P]
    dxs = [sm["dxa"], sm["dxb"]]
    scope = ExitStack()
    pm = [psum(b, scope, f"pm{P}0", [128, T]), psum(b, scope, f"pm{P}1", [128, T])]
    xst = [scope.enter_context(nc.sbuf_tensor(f"xs{P}0", [128, T], F32)),
           scope.enter_context(nc.sbuf_tensor(f"xs{P}1", [128, T], F32))]

    @block.sync
    def _(sy):
        entry_waits(sy, b, s0)
        for i in range(KC):
            if i >= 2:
                sy.wait_ge(sm["ve"].h, s0["ve"] + 2 * (i - 1))
            dxs[i % 2].inc(sy.dma_start(out=xst[i % 2][:, :],
                                        in_=b.di["x_fm"][i * 128:(i + 1) * 128, :]), 16)

    @block.tensor
    def _(t):
        t.wait_ge(sm["se"].h, s0["se"])
        t.wait_ge(sm["ve"].h, s0["ve"])
        for i in range(KC):
            if i >= 2:
                t.wait_ge(sm["ve"].h, s0["ve"] + 2 * (i - 1))
            if fidx < 4:
                tp = (32 * fidx, 0)
                rhs = b.a_pack[32 * fidx:32 * (fidx + 1), :]
                lw = b.w2s[32 * fidx:32 * (fidx + 1), i * 128:(i + 1) * 128]
            else:
                tp = (0, 0)
                rhs = b.a4[:, :]
                lw = b.w2s[0:32, C + i * 128: C + (i + 1) * 128]
            for h in range(2):
                mm = t.matmul(pm[i % 2][:, h * 512:(h + 1) * 512], lw,
                              rhs[:, h * 512:(h + 1) * 512],
                              start=True, stop=True, tile_position=tp)
            sm["te"].inc(mm, 1)

    @block.vector
    def _(v):
        for i in range(KC):
            v.wait_ge(sm["te"].h, s0["te"] + i + 1)
            v.wait_ge(dxs[i % 2].h, s0["dxa" if i % 2 == 0 else "dxb"] + 16 * (i // 2 + 1))
            x = xst[i % 2]
            v.tensor_tensor(b.xxt[:, 0:1], cst["xprev"][:, i:i + 1], x[:, 0:1], ALU.subtract)
            v.tensor_tensor(b.xxt[:, 1:T], x[:, 0:T - 1], x[:, 1:T], ALU.subtract)
            sm["ve"].inc(v.scalar_tensor_tensor(
                b.tmp0[:, :], pm[i % 2][:, :],
                cst["maa"][:, i * 6 + midx: i * 6 + midx + 1],
                b.xxt[:, :], ALU.add, ALU.mult), 1)
            sm["ve"].inc(v.tensor_tensor(dst[:, i * T:(i + 1) * T], x[:, :],
                                         b.tmp0[:, :], ALU.add), 1)

    scope.close()
    dump(b, "mix" + P, dst[:, :])


def ph_decay(b):
    nc, block, sm, cst = b.nc, b.block, b.sm, b.cst
    ph_mix(b, 0, "w", b.mixA)
    s0 = snap(b)
    scope = ExitStack()
    pw = psum(b, scope, "pw", [64, T])
    pd = [psum(b, scope, "pd0", [128, T]), psum(b, scope, "pd1", [128, T])]

    @block.tensor
    def _(t):
        t.wait_ge(sm["ve"].h, s0["ve"])
        for k in range(KC):
            lw = b.dw1s[:, k * 64:(k + 1) * 64]
            for h in range(2):
                mm = t.matmul(pw[:, h * 512:(h + 1) * 512], lw,
                              b.mixA[:, k * T + h * 512: k * T + (h + 1) * 512],
                              start=(k == 0), stop=(k == KC - 1))
        sm["te"].inc(mm, 1)
        t.wait_ge(sm["se"].h, s0["se"] + 1)
        for i in range(KC):
            if i >= 2:
                t.wait_ge(sm["se"].h, s0["se"] + i)
            lw = b.dw2s[:, i * 128:(i + 1) * 128]
            for h in range(2):
                mm = t.matmul(pd[i % 2][:, h * 512:(h + 1) * 512], lw,
                              b.tanhw[:, h * 512:(h + 1) * 512], start=True, stop=True)
            sm["te"].inc(mm, 1)

    @block.scalar
    def _(s):
        s.wait_ge(sm["te"].h, s0["te"] + 1)
        sm["se"].inc(s.activation(b.tanhw[:, :], pw[:, :], AF.Tanh), 1)
        for i in range(KC):
            s.wait_ge(sm["te"].h, s0["te"] + 1 + i + 1)
            if i >= 1:
                s.wait_ge(sm["ve"].h, s0["ve"] + 2 * i - 1)
            sm["se"].inc(s.activation(b.tmp0[:, :], pd[i % 2][:, :], AF.Exp,
                                      bias=cst["td"][:, i:i + 1]), 1)

    @block.vector
    def _(v):
        for i in range(KC):
            v.wait_ge(sm["se"].h, s0["se"] + 1 + i + 1)
            cch = cbch(b, i)
            for n in range(NCH):
                sc_i = v.tensor_tensor_scan(
                    cch[:, n * L:(n + 1) * L], b.tmp0[:, n * L:(n + 1) * L],
                    b.tmp0[:, n * L:(n + 1) * L], 0.0, ALU.add, ALU.bypass)
            sm["ve"].inc(sc_i, 1)
            v.drain()
            st = b.cstats
            base = i * (NCH * 6)
            c3 = cch.rearrange("p (n l) -> p n l", l=L)
            v.tensor_copy(st[:, base + 0 * NCH: base + 1 * NCH], c3[:, :, L // 2 - 1:L // 2])
            sm["ve"].inc(v.tensor_copy(st[:, base + 1 * NCH: base + 2 * NCH],
                                       c3[:, :, L - 1:L]), 1)

    s1 = snap(b)

    @block.vector
    def _(v):
        for i in range(KC):
            base = i * (NCH * 6)
            st = b.cstats
            if i >= 1:
                v.wait_ge(sm["se"].h, s1["se"] + i)
            v.tensor_tensor(b.tmp1[:, 0:NCH], st[:, base:base + NCH],
                            st[:, base + NCH: base + 2 * NCH], ALU.subtract)
            v.tensor_tensor_scan(b.tmp1[:, NCH:2 * NCH], st[:, base + NCH: base + 2 * NCH],
                                 st[:, base + NCH: base + 2 * NCH], 0.0, ALU.add, ALU.bypass)
            v.drain()
            v.tensor_copy(b.ctot[:, i:i + 1], b.tmp1[:, 2 * NCH - 1: 2 * NCH])
            v.tensor_tensor(b.tmp1[:, 3 * NCH:4 * NCH], b.tmp1[:, NCH:2 * NCH],
                            st[:, base + NCH:base + 2 * NCH], ALU.subtract)
            v.drain()
            sm["ve"].inc(v.tensor_tensor(b.tmp1[:, 4 * NCH:5 * NCH],
                                         b.tmp1[:, 3 * NCH:4 * NCH],
                                         st[:, base:base + NCH], ALU.add), 1)

    @block.scalar
    def _(s):
        for i in range(KC):
            base = i * (NCH * 6)
            st = b.cstats
            s.wait_ge(sm["ve"].h, s1["ve"] + i + 1)
            s.activation(st[:, base + 2 * NCH: base + 3 * NCH], b.tmp1[:, 0:NCH], AF.Exp)
            s.activation(st[:, base + 3 * NCH: base + 4 * NCH],
                         st[:, base + NCH: base + 2 * NCH], AF.Exp, scale=-1.0)
            s.activation(st[:, base + 4 * NCH: base + 5 * NCH],
                         st[:, base + 0 * NCH: base + 1 * NCH], AF.Exp, scale=-1.0)
            sm["se"].inc(s.activation(st[:, base + 5 * NCH: base + 6 * NCH],
                                      b.tmp1[:, 4 * NCH:5 * NCH], AF.Exp, scale=-1.0), 1)

    scope.close()
    dump(b, "cb_a", b.cb_a[:, :])
    dump(b, "cb_b", b.cb_b[:, :])
    dump(b, "cstats", b.cstats[:, :])
    dump(b, "ctot", b.ctot[:, :])


def ktc(b, i):
    if i < 8:
        return b.kt_a[:, i * T:(i + 1) * T]
    return b.kt_b[:, (i - 8) * T:(i - 8 + 1) * T]


def ydc(b, i):
    if i < 8:
        return b.yd_a[:, i * T:(i + 1) * T]
    return b.yd_b[:, (i - 8) * T:(i - 8 + 1) * T]


def cbch(b, o):
    if o < 8:
        return b.cb_a[:, o * T:(o + 1) * T]
    return b.cb_b[:, (o - 8) * T:(o - 8 + 1) * T]


def ph_proj_fm(b, P, dst_f, incl, orange=(0, KC)):
    nc, block, sm, cst = b.nc, b.block, b.sm, b.cst
    s0 = snap(b)
    dws = [sm["dwa"], sm["dwb"]]
    nv = 4 if P == "k" else 2
    ns = 2 if P == "k" else 1
    olist = list(range(*orange))
    scope = ExitStack()
    sfx = f"{P}{orange[0]}"
    pp = [psum(b, scope, f"pp{sfx}0", [128, T]), psum(b, scope, f"pp{sfx}1", [128, T])]
    wbuf = [scope.enter_context(nc.sbuf_tensor(f"wb{sfx}0", [128, C], BF16)),
            scope.enter_context(nc.sbuf_tensor(f"wb{sfx}1", [128, C], BF16))]

    @block.gpsimd
    def _(g):
        entry_waits(g, b, s0)
        for oi, o in enumerate(olist):
            if oi >= 2:
                g.wait_ge(sm["te"].h, s0["te"] + (oi - 1))
            src = b.di[f"W{P}L"][o * 128:(o + 1) * 128, :]
            dws[oi % 2].inc(g.dma_start(out=wbuf[oi % 2][:, :], in_=src), 16)

    @block.tensor
    def _(t):
        t.wait_ge(sm["ve"].h, s0["ve"])
        for oi, o in enumerate(olist):
            t.wait_ge(dws[oi % 2].h, s0["dwa" if oi % 2 == 0 else "dwb"] + 16 * (oi // 2 + 1))
            if oi >= 2:
                t.wait_ge(sm["ve"].h, s0["ve"] + nv * (oi - 1))
            for k in range(KC):
                lw = wbuf[oi % 2][:, k * 128:(k + 1) * 128]
                for h in range(2):
                    mm = t.matmul(pp[o % 2][:, h * 512:(h + 1) * 512], lw,
                                  b.mixA[:, k * T + h * 512: k * T + (h + 1) * 512],
                                  start=(k == 0), stop=(k == KC - 1))
            sm["te"].inc(mm, 1)

    def arg_incl(v, o):
        base = o * (NCH * 6)
        cch = cbch(b, o)
        cmid = b.cstats[:, base: base + NCH]
        for n in range(NCH):
            cm = cmid[:, n:n + 1]
            last = v.tensor_scalar(b.tmp1[:, n * L:(n + 1) * L], cch[:, n * L:(n + 1) * L],
                                   1.0, cm, ALU.mult, ALU.subtract)
        return last

    def arg_excl(v, o):
        base = o * (NCH * 6)
        cch = cbch(b, o)
        cmid = b.cstats[:, base: base + NCH]
        for n in range(NCH):
            cm = cmid[:, n:n + 1]
            v.tensor_scalar(b.tmp1[:, n * L: n * L + 1], cm, 1.0, None, ALU.mult)
            last = v.tensor_scalar(b.tmp1[:, n * L + 1:(n + 1) * L],
                                   cch[:, n * L:(n + 1) * L - 1], -1.0, cm, ALU.mult, ALU.add)
        return last

    @block.vector
    def _(v):
        for oi, o in enumerate(olist):
            dst = dst_f(o)
            if incl:
                sm["ve"].inc(arg_incl(v, o), 1)
            else:
                sm["ve"].inc(arg_excl(v, o), 1)
            v.wait_ge(sm["se"].h, s0["se"] + ns * oi + 1)
            v.wait_ge(sm["te"].h, s0["te"] + oi + 1)
            sm["ve"].inc(v.tensor_tensor(dst, pp[oi % 2][:, :], b.tmp2[:, :], ALU.mult), 1)
            if P == "k":
                sm["ve"].inc(arg_excl(v, o), 1)
                v.wait_ge(sm["se"].h, s0["se"] + ns * oi + 2)
                sm["ve"].inc(v.scalar_tensor_tensor(
                    ydc(b, o), pp[oi % 2][:, :],
                    cst["u"][:, o:o + 1], b.tmp2[:, :], ALU.mult, ALU.mult), 1)

    @block.scalar
    def _(s):
        for oi, o in enumerate(olist):
            s.wait_ge(sm["ve"].h, s0["ve"] + nv * oi + 1)
            sm["se"].inc(s.activation(b.tmp2[:, :], b.tmp1[:, :], AF.Exp), 1)
            if P == "k":
                s.wait_ge(sm["ve"].h, s0["ve"] + nv * oi + 3)
                sm["se"].inc(s.activation(b.tmp2[:, :], b.tmp1[:, :], AF.Exp, scale=-1.0), 1)

    scope.close()


def wq_chunks():
    out = []
    for i in range(KC):
        out.append(("gate_slice", i * 128, 0, 0))
    for i in range(KC):
        out.append(("val_slice", i * 128, 0, 1))
    for j in range(KC):
        out.append(("out_slice", 128 * (j // 2), 1024 * (j % 2), 2))
    return out


def ph_wq_sums(b):
    nc, block, sm, cst = b.nc, b.block, b.sm, b.cst
    s0 = snap(b)
    scope = ExitStack()
    pg = psum(b, scope, "pgam", [4, 1])
    wqst = [scope.enter_context(nc.sbuf_tensor("wqs0", [128, T], F32)),
            scope.enter_context(nc.sbuf_tensor("wqs1", [128, T], F32))]
    b.wqst = wqst
    chunks = wq_chunks()

    @block.sync
    def _(sy):
        sy.wait_ge(sm["dc"].h, s0["dc"])
        entry_waits(sy, b, s0)
        for idx, (nm, r0, c0, tag) in enumerate(chunks):
            if idx >= 2:
                sy.wait_ge(sm["ve"].h, s0["ve"] + (idx - 1))
            sm["dq"].inc(sy.dma_start(out=wqst[idx % 2][:, :],
                                      in_=b.di[nm][r0:r0 + 128, c0:c0 + 1024]), 16)

    @block.vector
    def _(v):
        for idx in range(48):
            v.wait_ge(sm["dq"].h, s0["dq"] + 16 * (idx + 1))
            sm["ve"].inc(v.tensor_reduce(b.tmp1[:, 40 + idx:41 + idx],
                                         wqst[idx % 2][:, :],
                                         AX.X, ALU.add, apply_absolute_value=True), 1)
        v.drain()
        v.tensor_reduce(b.tmp0[:, 0:1], b.tmp1[:, 40:56], AX.X, ALU.add)
        v.tensor_reduce(b.tmp0[:, 1:2], b.tmp1[:, 56:72], AX.X, ALU.add)
        v.tensor_reduce(b.tmp0[:, 2:3], b.tmp1[:, 72:88], AX.X, ALU.add)
        sm["ve"].inc(v.memset(b.tmp0[:, 3:4], 0.0), 1)

    @block.tensor
    def _(t):
        t.wait_ge(sm["ve"].h, s0["ve"] + 49)
        sm["te"].inc(t.matmul(pg[:, :], b.tmp0[:, 0:4], cst["ones"][:, 0:1],
                              start=True, stop=True), 1)

    @block.vector
    def _(v):
        v.wait_ge(sm["te"].h, s0["te"] + 1)
        sm["ve"].inc(v.tensor_copy(b.tmp0[0:4, 10:11], pg[:, :]), 1)

    @block.gpsimd
    def _(g):
        g.wait_ge(sm["ve"].h, s0["ve"] + 50)
        sm["dga"].inc(g.dma_start(out=b.cc_gam_in.ap().rearrange("o f -> f o"),
                                  in_=b.tmp0[0:4, 10:11]), 16)
        g.wait_ge(sm["dga"].h, sm["dga"].n)
        sm["cc"].inc(g.collective_compute(
            "AllReduce", ALU.add, replica_groups=[list(range(NCORES))],
            ins=[b.cc_gam_in.ap().opt()], outs=[b.cc_gam_out.ap().opt()]), 1)

    scope.close()


def ph_wq_ternary(b):
    nc, block, sm, cst = b.nc, b.block, b.sm, b.cst
    s0 = snap(b)
    scope = ExitStack()
    pb = psum(b, scope, "pthr", [128, 4])
    wqst = [scope.enter_context(nc.sbuf_tensor("wqt0", [128, T], F32)),
            scope.enter_context(nc.sbuf_tensor("wqt1", [128, T], F32))]
    wqo = [scope.enter_context(nc.sbuf_tensor("wqq0", [128, T], BF16)),
           scope.enter_context(nc.sbuf_tensor("wqq1", [128, T], BF16))]
    NELEM = float(C * HID)
    chunks = wq_chunks()

    @block.sync
    def _(sy):
        sy.wait_ge(sm["cc"].h, s0["cc"])
        sm["dq"].inc(sy.dma_start(out=b.gamsb[:, :], in_=b.cc_gam_out.ap()), 16)

    @block.tensor
    def _(t):
        t.wait_ge(sm["dq"].h, s0["dq"] + 16)
        sm["te"].inc(t.matmul(pb[:, :], cst["ones"][0:1, 0:128], b.gamsb[0:1, :],
                              start=True, stop=True), 1)

    @block.vector
    def _(v):
        v.wait_ge(sm["te"].h, s0["te"] + 1)
        v.memset(b.thr[:, :], 0.0)
        v.memset(b.thrn[:, :], 0.0)
        for mi in range(3):
            v.tensor_scalar(b.thr[:, mi:mi + 1], pb[:, mi:mi + 1],
                            1.0 / NELEM, 1e-5, ALU.mult, ALU.max)
        v.drain()
        v.tensor_scalar(b.thr[:, 4:7], b.thr[:, 0:3], 0.5, None, ALU.mult)
        v.drain()
        sm["ve"].inc(v.tensor_scalar(b.thrn[:, 4:7], b.thr[:, 4:7], -1.0, None, ALU.mult), 1)

    s1 = snap(b)

    @block.sync
    def _(sy):
        entry_waits(sy, b, s1)
        for idx, (nm, r0, c0, tag) in enumerate(chunks):
            if idx >= 2:
                sy.wait_ge(sm["ve"].h, s1["ve"] + 2 * (idx - 1))
            sm["dq"].inc(sy.dma_start(out=wqst[idx % 2][:, :],
                                      in_=b.di[nm][r0:r0 + 128, c0:c0 + 1024]), 16)

    @block.vector
    def _(v):
        v.wait_ge(sm["ve"].h, s1["ve"])
        for idx, (nm, r0, c0, tag) in enumerate(chunks):
            v.wait_ge(sm["dq"].h, s1["dq"] + 16 * (idx + 1))
            if idx >= 2:
                v.wait_ge(sm["dgb"].h, s1["dgb"] + 16 * nstores(idx - 2))
            st = wqst[idx % 2][:, :]
            v.tensor_scalar(b.tmp1[:, 0:T], st, b.thrn[:, 4 + tag:5 + tag], None, ALU.is_lt)
            sm["ve"].inc(v.scalar_tensor_tensor(
                wqo[idx % 2][:, :], st, b.thr[:, 4 + tag:5 + tag],
                b.tmp1[:, 0:T], ALU.is_gt, ALU.subtract), 2)

    @block.gpsimd
    def _(g):
        wq = b.cc_wq_in.ap()
        for idx, (nm, r0, c0, tag) in enumerate(chunks):
            g.wait_ge(sm["ve"].h, s1["ve"] + 2 * (idx + 1))
            qv = wqo[idx % 2][:, :]
            if tag < 2:
                i = r0 // 128
                for ml in range(8):
                    off = tag * (1024 * C) + ml * (128 * 2048)
                    dst = wq[0:1, off:off + 128 * 2048] \
                        .rearrange("o (p kk) -> (o p) kk", p=128)[:, i * 128:(i + 1) * 128]
                    sm["dgb"].inc(g.dma_start(out=dst, in_=qv[:, ml * 128:(ml + 1) * 128]), 16)
            else:
                kl = r0 // 128
                off = 2 * (1024 * C) + kl * (128 * C)
                dst = wq[0:1, off:off + 128 * C].rearrange("o (p j) -> (o p) j", p=128)[:, c0:c0 + 1024]
                sm["dgb"].inc(g.dma_start(out=dst, in_=qv), 16)
        g.wait_ge(sm["dgb"].h, sm["dgb"].n)
        sm["cc"].inc(g.collective_compute(
            "AllGather", ALU.bypass, replica_groups=[list(range(NCORES))],
            ins=[b.cc_wq_in.ap().opt()], outs=[b.cc_wq_out.ap().opt()]), 1)

    dump(b, "thr", b.thr[:, :])
    scope.close()


def nstores(upto_idx):
    n = 0
    for idx, (nm, r0, c0, tag) in enumerate(wq_chunks()):
        if idx > upto_idx:
            break
        n += 8 if tag < 2 else 1
    return n


def ph_vproj(b):
    nc, block, sm = b.nc, b.block, b.sm
    s0 = snap(b)
    dws = [sm["dwa"], sm["dwb"]]
    scope = ExitStack()
    pv = [psum(b, scope, f"pv{j}", [128, 512]) for j in range(8)]
    wbuf = [scope.enter_context(nc.sbuf_tensor("wbv0", [128, C], BF16)),
            scope.enter_context(nc.sbuf_tensor("wbv1", [128, C], BF16))]

    @block.gpsimd
    def _(g):
        entry_waits(g, b, s0)
        for tg in range(4):
            for k in range(KC):
                gi = tg * KC + k
                if gi >= 2:
                    g.wait_ge(sm["te"].h, s0["te"] + gi - 1)
                src = b.di["WvT"][k * 128:(k + 1) * 128, :]
                dws[gi % 2].inc(g.dma_start(out=wbuf[gi % 2][:, :], in_=src), 16)

    @block.tensor
    def _(t):
        t.wait_ge(sm["ve"].h, s0["ve"])
        for tg in range(4):
            if tg >= 1:
                t.wait_ge(sm["ve"].h, s0["ve"] + tg)
            for k in range(KC):
                gi = tg * KC + k
                t.wait_ge(dws[gi % 2].h,
                          s0["dwa" if gi % 2 == 0 else "dwb"] + 16 * (gi // 2 + 1))
                for tt in range(2):
                    ttg = tg * 2 + tt
                    lhs = b.mixA[:, k * T + ttg * 128: k * T + (ttg + 1) * 128]
                    for o in range(4):
                        mm = t.matmul(pv[tt * 4 + o][:, :], lhs,
                                      wbuf[gi % 2][:, o * 512:(o + 1) * 512],
                                      start=(k == 0), stop=(k == KC - 1))
                sm["te"].inc(mm, 1)

    @block.vector
    def _(v):
        for tg in range(4):
            v.wait_ge(sm["te"].h, s0["te"] + (tg + 1) * KC)
            for tt in range(2):
                ttg = tg * 2 + tt
                for o in range(4):
                    last = (tt == 1 and o == 3)
                    ins = v.tensor_copy(b.vt[:, ttg * C + o * 512: ttg * C + (o + 1) * 512],
                                        pv[tt * 4 + o][:, :])
                    if last:
                        sm["ve"].inc(ins, 1)

    scope.close()



def ph_wkv(b):
    nc, block, sm, cst = b.nc, b.block, b.sm, b.cst
    s0 = snap(b)
    scope = ExitStack()
    pAU = [psum(b, scope, f"pAU{j}", [128, 512]) for j in range(2)]
    pYS = [psum(b, scope, f"pYS{j}", [128, 512]) for j in range(2)]
    pTt = [psum(b, scope, f"pTt{j}", [128, 2 * L], BF16) for j in range(2)]
    pA = [pAU[0][:, 0:128], pAU[1][:, 0:128]]
    pU = [pAU[0][:, 128:256], pAU[1][:, 128:256]]
    pY = [pYS[0][:, 0:64], pYS[1][:, 0:64]]
    pS = [pYS[0][:, 64:128], pYS[1][:, 64:128]]
    pY2 = [pYS[0][:, 128:192], pYS[1][:, 128:192]]
    pT = pTt

    @block.vector
    def _(v):
        v.wait_ge(sm["se"].h, s0["se"])
        v.wait_ge(sm["ve"].h, s0["ve"])
        sm["ve"].inc(v.memset(b.Spack[:, :], 0.0), 1)
        for e in range(WKV_EMAX if WKV_EMAX else NCH * KC):
            n, p = divmod(e, KC)
            rows = (n % 2) * 64
            base = p * (NCH * 6)
            ktile = ktc(b, p)[:, n * L:(n + 1) * L]
            par = e % 2
            v.tensor_scalar(b.s0s[:, :], b.Spack[:, p * L:(p + 1) * L],
                            b.cstats[:, base + 4 * NCH + n: base + 4 * NCH + n + 1],
                            None, ALU.mult)
            sm["ve"].inc(v.tensor_scalar(b.khf[:, :], ktile,
                                         b.cstats[:, base + 2 * NCH + n: base + 2 * NCH + n + 1],
                                         None, ALU.mult), 1)
            v.wait_ge(sm["te"].h, s0["te"] + 2 * e + 1)
            ms = cst["masksel"]
            v.tensor_tensor(b.tsel[rows:rows + 64, :], pA[par][rows:rows + 64, :],
                            ms[rows:rows + 64, 0:2 * L], ALU.mult)
            v.drain()
            v.select(b.asel[rows:rows + 64, :], cst["maskd8"][rows:rows + 64, :],
                     pU[par][rows:rows + 64, :], b.tsel[rows:rows + 64, :])
            sm["ve"].inc(v.tensor_copy(b.kht[rows:rows + 64, :], pT[par][rows:rows + 64, :]), 1)
            v.wait_ge(sm["te"].h, s0["te"] + 2 * e + 2)
            ytile = ydc(b, p)[:, n * L:(n + 1) * L]
            v.tensor_copy(ytile, pY[par][:, :])
            v.drain()
            v.tensor_tensor(ytile, ytile, pY2[par][:, :], ALU.add)
            sm["ve"].inc(v.scalar_tensor_tensor(
                b.Spack[:, p * L:(p + 1) * L], b.Spack[:, p * L:(p + 1) * L],
                b.cstats[:, base + 3 * NCH + n: base + 3 * NCH + n + 1],
                pS[par][:, :], ALU.mult, ALU.add), 1)

    @block.tensor
    def _(t):
        t.wait_ge(sm["ve"].h, s0["ve"] + 1)
        for e in range(WKV_EMAX if WKV_EMAX else NCH * KC):
            n, p = divmod(e, KC)
            rows = (n % 2) * 64
            par = e % 2
            tt = n // 2
            ktile = ktc(b, p)[:, n * L:(n + 1) * L]
            rtile = b.rt[:, p * T + n * L: p * T + (n + 1) * L]
            uktile = ydc(b, p)[:, n * L:(n + 1) * L]
            t.wait_ge(sm["ve"].h, s0["ve"] + 1 + 3 * e + 1)
            if e >= 2:
                t.wait_ge(sm["ve"].h, s0["ve"] + 1 + 3 * (e - 2) + 3)
            vt_ = lambda h: b.vt[rows:rows + 64,
                                 tt * C + (2 * p + h) * 64: tt * C + (2 * p + h + 1) * 64]
            mmA = lambda h: t.matmul(pA[par][rows:rows + 64, h * L:(h + 1) * L],
                                     ktile[h * 64:h * 64 + 64, :],
                                     rtile[h * 64:h * 64 + 64, :],
                                     start=True, stop=True, tile_position=(h * 64, rows))
            mmU = lambda h: t.matmul(pU[par][rows:rows + 64, h * L:(h + 1) * L],
                                     uktile[h * 64:h * 64 + 64, :],
                                     rtile[h * 64:h * 64 + 64, :],
                                     start=True, stop=True, tile_position=(h * 64, rows))
            mmY = lambda h: t.matmul(pY[par][h * 64:h * 64 + 64, :], vt_(h),
                                     b.asel[rows:rows + 64, h * L:(h + 1) * L],
                                     start=True, stop=True, tile_position=(rows, h * 64))
            mmS = lambda h: t.matmul(pS[par][h * 64:h * 64 + 64, :],
                                     b.kht[rows:rows + 64, h * L:(h + 1) * L],
                                     vt_(h), start=True, stop=True,
                                     tile_position=(rows, h * 64))
            mmY2 = lambda h: t.matmul(pY2[par][h * 64:h * 64 + 64, :],
                                      b.s0s[h * 64:h * 64 + 64, :],
                                      rtile[h * 64:h * 64 + 64, :],
                                      start=True, stop=True,
                                      tile_position=(h * 64, h * 64))

            def drain(mm):
                sm["tdr"].inc(mm, 1)
                t.wait_ge(sm["tdr"].h, sm["tdr"].n)

            mmA(0)
            mmU(0)
            drain(t.transpose(pT[par][rows:rows + 64, :], b.khf[:, :],
                              cst["ident"][:, :], tile_position=(0, rows)))
            mmA(1)
            sm["te"].inc(mmU(1), 1)
            if rows == 0:
                drain(mmY2(1))
                t.wait_ge(sm["ve"].h, s0["ve"] + 1 + 3 * e + 2)
                mmY(0); mmS(0); mmY(1); mmS(1)
                sm["te"].inc(mmY2(0), 1)
            else:
                t.wait_ge(sm["ve"].h, s0["ve"] + 1 + 3 * e + 2)
                mmY(0); mmS(0); mmY(1); mmS(1)
                drain(mmY2(1))
                sm["te"].inc(mmY2(0), 1)

    scope.close()
    dump(b, "Spack", b.Spack[:, :])
    dump(b, "yl_a", b.yd_a[:, :])
    dump(b, "yl_b", b.yd_b[:, :])


def ph_publish(b):
    nc, block, sm = b.nc, b.block, b.sm
    s0 = snap(b)
    scope = ExitStack()
    pt = psum(b, scope, "ptc", [16, 128])

    @block.tensor
    def _(t):
        t.wait_ge(sm["ve"].h, s0["ve"])
        sm["te"].inc(t.transpose(pt[:, :], b.ctot[:, :], cst_ident_f32(b)), 1)

    @block.vector
    def _(v):
        v.wait_ge(sm["te"].h, s0["te"] + 1)
        sm["ve"].inc(v.tensor_copy(b.ctrow[:, :], pt[:, :]), 1)

    @block.sync
    def _(sy):
        sy.wait_ge(sm["ve"].h, s0["ve"] + 1)
        dst = b.cc_s_in.ap()[0:1, 0:128 * T].rearrange("o (p t) -> (o p) t", p=128)
        sm["dq"].inc(sy.dma_start(out=dst, in_=b.Spack[:, :]), 16)
        dst = b.cc_s_in.ap()[0:1, 128 * T:].rearrange("o (n p) -> (o n) p", p=128)
        sm["dq"].inc(sy.dma_start(out=dst, in_=b.ctrow[:, :]), 16)

    @block.gpsimd
    def _(g):
        g.wait_ge(sm["dq"].h, s0["dq"] + 32)
        sm["cc"].inc(g.collective_compute(
            "AllGather", ALU.bypass, replica_groups=[list(range(NCORES))],
            ins=[b.cc_s_in.ap().opt()], outs=[b.cc_s_out.ap().opt()]), 1)

    scope.close()


def cst_ident_f32(b):
    return b.cst["identf"][:, :]


def ph_gproj(b):
    nc, block, sm = b.nc, b.block, b.sm
    s0 = snap(b)
    dws = [sm["dwa"], sm["dwb"]]
    scope = ExitStack()
    pp = [psum(b, scope, "ppg0", [128, T]), psum(b, scope, "ppg1", [128, T])]
    wbuf = [scope.enter_context(nc.sbuf_tensor("wbg0", [128, C], BF16)),
            scope.enter_context(nc.sbuf_tensor("wbg1", [128, C], BF16))]

    @block.gpsimd
    def _(g):
        entry_waits(g, b, s0)
        for o in range(KC):
            if o >= 2:
                g.wait_ge(sm["se"].h, s0["se"] + (o - 1))
            src = b.di["WgL"][o * 128:(o + 1) * 128, :]
            dws[o % 2].inc(g.dma_start(out=wbuf[o % 2][:, :], in_=src), 16)

    @block.tensor
    def _(t):
        entry_waits(t, b, s0)
        for o in range(KC):
            t.wait_ge(dws[o % 2].h, s0["dwa" if o % 2 == 0 else "dwb"] + 16 * (o // 2 + 1))
            if o >= 2:
                t.wait_ge(sm["se"].h, s0["se"] + (o - 1))
            for k in range(KC):
                lw = wbuf[o % 2][:, k * 128:(k + 1) * 128]
                for h in range(2):
                    mm = t.matmul(pp[o % 2][:, h * 512:(h + 1) * 512], lw,
                                  b.mixA[:, k * T + h * 512: k * T + (h + 1) * 512],
                                  start=(k == 0), stop=(k == KC - 1))
            sm["te"].inc(mm, 1)

    @block.scalar
    def _(s):
        for o in range(KC):
            s.wait_ge(sm["te"].h, s0["te"] + o + 1)
            sm["se"].inc(s.activation(ktc(b, o), pp[o % 2][:, :], AF.Silu), 1)

    scope.close()
    dump(b, "gb_a", b.kt_a[:, :])



def ph_combine_correct(b):
    nc, block, sm, cst = b.nc, b.block, b.sm, b.cst
    s0 = snap(b)
    scope = ExitStack()
    pL = psum(b, scope, "pL", [128, 8])
    pC = [psum(b, scope, f"pC{j}", [128, 512]) for j in range(2)]
    b.sinb = scope.enter_context(nc.sbuf_tensor("sinb", [128, T], BF16))
    b.w8 = scope.enter_context(nc.sbuf_tensor("w8", [128, KC * 8], F32))
    b.ctot9 = scope.enter_context(nc.sbuf_tensor("ctot9", [9, C], F32))
    b.sgst = [scope.enter_context(nc.sbuf_tensor("sg0", [128, T], F32)),
              scope.enter_context(nc.sbuf_tensor("sg1", [128, T], F32))]

    @block.sync
    def _(sy):
        sy.wait_ge(sm["cc"].h, s0["cc"])
        entry_waits(sy, b, s0)
        for e in range(NCORES):
            off = 128 * T
            src = b.cc_s_out.ap()[e:e + 1, off:off + C]
            sm["dq"].inc(sy.dma_start(out=b.ctot9[e:e + 1, :], in_=src), 16)

    @block.sync
    def _(sy):
        sm["dq"].inc(sy.dma_start(out=b.ctot9[8:9, :].rearrange("o (a c) -> o a c", a=4),
                                  in_=cst["ones"][0:4, 0:512]), 16)

    @block.vector
    def _(v):
        sm["ve"].inc(v.memset(b.vnop[:, :], 0.0), 1)

    @block.tensor
    def _(t):
        t.wait_ge(sm["dq"].h, s0["dq"] + 16 * (NCORES + 1))
        t.wait_ge(sm["ve"].h, s0["ve"] + 1)
        t.wait_ge(sm["se"].h, s0["se"])
        for i in range(KC):
            if i >= 1:
                t.wait_ge(sm["se"].h, s0["se"] + i)
            sm["te"].inc(t.matmul(pL[:, :], b.ctot9[:, i * 128:(i + 1) * 128],
                                  cst["m2aug"][:, :], start=True, stop=True), 1)

    @block.scalar
    def _(s):
        for i in range(KC):
            s.wait_ge(sm["te"].h, s0["te"] + i + 1)
            sm["se"].inc(s.activation(b.w8[:, i * 8:(i + 1) * 8], pL[:, :], AF.Exp,
                                      scale=-1.0), 1)

    s1 = snap(b)

    @block.sync
    def _(sy):
        entry_waits(sy, b, s1)
        for e in range(NCORES):
            if e >= 2:
                sy.wait_ge(sm["ve"].h, s1["ve"] + (e - 1))
            src = b.cc_s_out.ap()[e:e + 1, 0:128 * T].rearrange("o (p t) -> (o p) t", p=128)
            (sm["dxa"] if e % 2 == 0 else sm["dxb"]).inc(
                sy.dma_start(out=b.sgst[e % 2][:, :], in_=src), 16)

    @block.vector
    def _(v):
        v.wait_ge(sm["se"].h, s1["se"])
        for e in range(NCORES):
            v.wait_ge((sm["dxa"] if e % 2 == 0 else sm["dxb"]).h,
                      s1["dxa" if e % 2 == 0 else "dxb"] + 16 * (e // 2 + 1))
            for i in range(KC):
                w8c = b.w8[:, i * 8 + e: i * 8 + e + 1]
                seg = slice(i * L, (i + 1) * L)
                if e == 0:
                    v.tensor_scalar(b.tmp2[:, seg], b.sgst[0][:, seg], w8c, None, ALU.mult)
                else:
                    last = v.scalar_tensor_tensor(b.tmp2[:, seg], b.sgst[e % 2][:, seg],
                                                  w8c, b.tmp2[:, seg], ALU.mult, ALU.add)
            sm["ve"].inc(last if e > 0 else v.memset(b.vnop[:, :], 0.0), 1)
        sm["ve"].inc(v.tensor_copy(b.sinb[:, :], b.tmp2[:, :]), 1)
        for p in range(KC):
            base = p * (NCH * 6)
            for n in range(NCH):
                col = slice(p * T + n * L, p * T + (n + 1) * L)
                last = v.tensor_scalar(
                    b.mixA[:, col], b.rt[:, col],
                    b.cstats[:, base + 5 * NCH + n: base + 5 * NCH + n + 1],
                    None, ALU.mult)
            sm["ve"].inc(last, 1)

    s2 = snap(b)

    @block.tensor
    def _(t):
        t.wait_ge(sm["ve"].h, s2["ve"])
        for p in range(KC):
            for th in range(2):
                par = th
                if p > 0 or th > 0:
                    t.wait_ge(sm["ve"].h, s2["ve"] + 2 * p + th - 1)
                    t.wait_ge(sm["te"].h, s2["te"] + 2 * p + th)
                mm0 = t.matmul(pC[par][0:64, :],
                               b.sinb[0:64, p * L:(p + 1) * L],
                               b.mixA[0:64, p * T + th * 512: p * T + (th + 1) * 512],
                               start=True, stop=True, tile_position=(0, 0))
                sm["tdr"].inc(mm0, 1)
                t.wait_ge(sm["tdr"].h, sm["tdr"].n)
                mm = t.matmul(pC[par][64:128, :],
                              b.sinb[64:128, p * L:(p + 1) * L],
                              b.mixA[64:128, p * T + th * 512: p * T + (th + 1) * 512],
                              start=True, stop=True, tile_position=(64, 64))
                sm["te"].inc(mm, 1)

    @block.vector
    def _(v):
        for p in range(KC):
            for th in range(2):
                v.wait_ge(sm["te"].h, s2["te"] + 2 * p + th + 1)
                ych = ydc(b, p)[:, th * 512:(th + 1) * 512]
                sm["ve"].inc(v.tensor_tensor(ych, ych, pC[th][:, :], ALU.add), 1)

    scope.close()

    dump(b, "sin", b.sinb[:, :])
    dump(b, "ctot9", b.ctot9[:, :])
    dump(b, "w8", b.w8[:, :])
    dump(b, "yc_a", b.yd_a[:, :])
    dump(b, "yc_b", b.yd_b[:, :])
    dump(b, "w8", b.w8[:, :])


def ph_gn(b):
    nc, block, sm, cst = b.nc, b.block, b.sm, b.cst
    s0 = snap(b)
    scope = ExitStack()
    pS1 = psum(b, scope, "pgn1", [32, T])
    pS2 = psum(b, scope, "pgn2", [32, T])
    pA0 = [psum(b, scope, f"pa0{j}", [128, 512]) for j in range(2)]
    pB0 = [psum(b, scope, f"pb0{j}", [128, 512]) for j in range(2)]
    ysq = scope.enter_context(nc.sbuf_tensor("ysq", [128, KC * T], BF16))
    stat32 = scope.enter_context(nc.sbuf_tensor("stat32", [32, 3 * T], F32))
    statbf = scope.enter_context(nc.sbuf_tensor("statbf", [32, 2 * T], BF16))
    f_all = scope.enter_context(nc.sbuf_tensor("f_all_sb", [32, KC * 128], BF16))
    b.stat32 = stat32

    @block.sync
    def _(sy):
        entry_waits(sy, b, s0) if False else None
        for k in ("te", "ve", "se"):
            if b.sm[k].n:
                sy.wait_ge(b.sm[k].h, b.sm[k].n)
        sm["dc"].inc(sy.dma_start(out=f_all[:, :], in_=b.di["f_all"][:, :]), 16)

    @block.scalar
    def _(s):
        s.wait_ge(sm["ve"].h, s0["ve"])
        for i in range(KC):
            sm["se"].inc(s.activation(ysq[:, i * T:(i + 1) * T], ydc(b, i), AF.Square), 1)

    @block.tensor
    def _(t):
        t.wait_ge(sm["se"].h, s0["se"] + KC)
        for i in range(KC):
            E = cst["e_all"][:, i * 32:(i + 1) * 32]
            for th in range(2):
                t.matmul(pS1[:, th * 512:(th + 1) * 512], E,
                         ydc(b, i)[:, th * 512:(th + 1) * 512],
                         start=(i == 0), stop=(i == KC - 1))
                mm = t.matmul(pS2[:, th * 512:(th + 1) * 512], E,
                              ysq[:, i * T + th * 512: i * T + (th + 1) * 512],
                              start=(i == 0), stop=(i == KC - 1))
        sm["te"].inc(mm, 1)

    @block.vector
    def _(v):
        v.wait_ge(sm["te"].h, s0["te"] + 1)
        st = stat32
        v.tensor_scalar(st[:, 0:T], pS1[:, :], 1.0 / 64.0, None, ALU.mult)
        v.tensor_scalar(st[:, T:2 * T], pS2[:, :], 1.0 / 64.0, None, ALU.mult)
        v.tensor_tensor(b.tmp0[0:32, 0:T], st[:, 0:T], st[:, 0:T], ALU.mult)
        sm["ve"].inc(v.tensor_tensor(st[:, 2 * T:3 * T], st[:, T:2 * T],
                                     b.tmp0[0:32, 0:T], ALU.subtract), 1)

    @block.scalar
    def _(s):
        s.wait_ge(sm["ve"].h, s0["ve"] + 1)
        sm["se"].inc(s.activation(b.tmp0[0:32, 0:T], stat32[:, 2 * T:3 * T],
                                  AF.Sqrt, bias=cst["epsv"][0:32, 0:1]), 1)

    @block.vector
    def _(v):
        v.wait_ge(sm["se"].h, s0["se"] + KC + 1)
        v.reciprocal(stat32[:, 2 * T:3 * T], b.tmp0[0:32, 0:T])
        v.tensor_tensor(stat32[:, T:2 * T], stat32[:, 0:T],
                        stat32[:, 2 * T:3 * T], ALU.mult)
        v.tensor_copy(statbf[:, 0:T], stat32[:, 2 * T:3 * T])
        sm["ve"].inc(v.tensor_copy(statbf[:, T:2 * T], stat32[:, T:2 * T]), 1)

    s1 = snap(b)

    @block.tensor
    def _(t):
        t.wait_ge(sm["ve"].h, s1["ve"])
        t.wait_ge(sm["dc"].h, sm["dc"].n)
        for idx in range(2 * KC):
            i, th = divmod(idx, 2)
            par = idx % 2
            if idx >= 2:
                t.wait_ge(sm["ve"].h, s1["ve"] + (idx - 1))
            F = f_all[:, i * 128:(i + 1) * 128]
            t.matmul(pA0[par][:, :], F,
                     statbf[:, th * 512:(th + 1) * 512],
                     start=True, stop=True)
            mm = t.matmul(pB0[par][:, :], F,
                          statbf[:, T + th * 512: T + (th + 1) * 512],
                          start=True, stop=True)
            sm["te"].inc(mm, 1)

    @block.vector
    def _(v):
        for idx in range(2 * KC):
            i, th = divmod(idx, 2)
            par = idx % 2
            v.wait_ge(sm["te"].h, s1["te"] + idx + 1)
            ych = ydc(b, i)[:, th * 512:(th + 1) * 512]
            gch = ktc(b, i)[:, th * 512:(th + 1) * 512]
            v.tensor_tensor(b.tmp0[:, 0:512], ych, pA0[par][:, :], ALU.mult)
            v.tensor_tensor(b.tmp0[:, 0:512], b.tmp0[:, 0:512], pB0[par][:, :], ALU.subtract)
            v.tensor_scalar(b.tmp0[:, 0:512], b.tmp0[:, 0:512],
                            cst["lng"][:, i:i + 1], cst["lnb"][:, i:i + 1],
                            ALU.mult, ALU.add)
            sm["ve"].inc(v.tensor_tensor(ych, b.tmp0[:, 0:512], gch, ALU.mult), 1)

    scope.close()



def ph_wo_ln1(b):
    nc, block, sm, cst = b.nc, b.block, b.sm, b.cst
    s0 = snap(b)
    dws = [sm["dwa"], sm["dwb"]]
    scope = ExitStack()
    pW = [psum(b, scope, f"pw{j}", [128, 512]) for j in range(8)]
    stg = scope.enter_context(nc.sbuf_tensor("stg", [128, 8192], F32))
    b.q1tm = scope.enter_context(nc.sbuf_tensor("q1tm", [128, 2048], BF16))
    wbuf = [scope.enter_context(nc.sbuf_tensor("wbo0", [128, C], BF16)),
            scope.enter_context(nc.sbuf_tensor("wbo1", [128, C], BF16))]
    xtm0 = scope.enter_context(nc.sbuf_tensor("xtm0", [128, C], F32))
    xtm = [xtm0, xtm0]
    x1pre = stg[:, 0:2048]
    x1r = stg[:, 2048:4096]
    sqd = stg[:, 4096:6144]
    v1q = stg[:, 6144:8192]
    cvec = b.tmp1

    @block.gpsimd
    def _(g):
        entry_waits(g, b, s0)
        for tg in range(4):
            for k in range(KC):
                gi = tg * KC + k
                if gi >= 2:
                    g.wait_ge(sm["te"].h, s0["te"] + (gi - 1))
                src = b.di["WoT"][k * 128:(k + 1) * 128, :]
                dws[gi % 2].inc(g.dma_start(out=wbuf[gi % 2][:, :], in_=src), 16)

    @block.sync
    def _(sy):
        entry_waits(sy, b, s0)
        for tt in range(NTT):
            if tt >= 1:
                sy.wait_ge(sm["ve"].h, s0["ve"] + 4 * (tt - 1) + 1)
            src = b.di["x_tm"][tt * 128:(tt + 1) * 128, :]
            (sm["dxa"] if tt % 2 == 0 else sm["dxb"]).inc(
                sy.dma_start(out=xtm[tt % 2][:, :], in_=src), 16)
            if tt >= 1:
                sy.wait_ge(sm["ve"].h, s0["ve"] + 4 * (tt - 1) + 3)
                sm["dq"].inc(sy.dma_start(out=b.x1_scr[(tt - 1) * 128:tt * 128, :],
                                          in_=x1r), 16)
        sy.wait_ge(sm["ve"].h, s0["ve"] + 4 * (NTT - 1) + 3)
        sm["dq"].inc(sy.dma_start(out=b.x1_scr[(NTT - 1) * 128:NTT * 128, :],
                                  in_=x1r), 16)


    @block.tensor
    def _(t):
        t.wait_ge(sm["ve"].h, s0["ve"])
        for tg in range(4):
            if tg >= 1:
                t.wait_ge(sm["ve"].h, s0["ve"] + 4 * (2 * tg - 1) + 1)
            for k in range(KC):
                gi = tg * KC + k
                t.wait_ge(dws[gi % 2].h,
                          s0["dwa" if gi % 2 == 0 else "dwb"] + 16 * (gi // 2 + 1))
                for ttl in range(2):
                    tt = tg * 2 + ttl
                    lhs = ydc(b, k)[:, tt * 128:(tt + 1) * 128]
                    for o in range(4):
                        mm = t.matmul(pW[ttl * 4 + o][:, :], lhs,
                                      wbuf[gi % 2][:, o * 512:(o + 1) * 512],
                                      start=(k == 0), stop=(k == KC - 1))
                sm["te"].inc(mm, 1)

    @block.vector
    def _(v):
        for tt in range(NTT):
            tg, ttl = divmod(tt, 2)
            v.wait_ge(sm["te"].h, s0["te"] + 16 * (tg + 1))
            v.wait_ge((sm["dxa"] if tt % 2 == 0 else sm["dxb"]).h,
                      s0["dxa" if tt % 2 == 0 else "dxb"] + 16 * (tt // 2 + 1))
            if tt >= 1:
                v.wait_ge(sm["dq"].h, s0["dq"] + 16 * tt)
                v.wait_ge(sm["vq"].h, s0["vq"] + 256 * tt)
            for o in range(4):
                v.tensor_tensor(x1pre[:, o * 512:(o + 1) * 512], pW[ttl * 4 + o][:, :],
                                xtm[tt % 2][:, o * 512:(o + 1) * 512], ALU.add)
            sm["ve"].inc(v.tensor_reduce(cvec[:, 0:1], x1pre, AX.X, ALU.add), 1)
            v.wait_ge(sm["se"].h, s0["se"] + 2 * tt + 1)
            v.drain()
            v.tensor_scalar(cvec[:, 2:3], cvec[:, 0:1], 1.0 / 2048.0, None, ALU.mult)
            v.tensor_scalar(cvec[:, 3:4], cvec[:, 1:2], 1.0 / 2048.0, None, ALU.mult)
            v.drain()
            v.tensor_tensor(cvec[:, 4:5], cvec[:, 2:3], cvec[:, 2:3], ALU.mult)
            v.drain()
            sm["ve"].inc(v.tensor_tensor(cvec[:, 4:5], cvec[:, 3:4], cvec[:, 4:5],
                                         ALU.subtract), 1)
            v.wait_ge(sm["se"].h, s0["se"] + 2 * tt + 2)
            v.reciprocal(cvec[:, 6:7], cvec[:, 5:6])
            v.drain()
            v.tensor_scalar(x1r, x1pre, cvec[:, 2:3], cvec[:, 6:7], ALU.subtract, ALU.mult)
            v.tensor_reduce(cvec[:, 7:8], x1r, AX.X, ALU.max, apply_absolute_value=True)
            v.drain()
            v.tensor_scalar(cvec[:, 7:8], cvec[:, 7:8], 1e-5, None, ALU.max)
            v.drain()
            v.reciprocal(cvec[:, 9:10], cvec[:, 7:8])
            v.drain()
            v.tensor_scalar(cvec[:, 10:11], cvec[:, 9:10], 127.0, None, ALU.mult)
            sm["ve"].inc(v.tensor_scalar(b.sx1[:, tt:tt + 1], cvec[:, 7:8],
                                         1.0 / 127.0, None, ALU.mult), 1)
            v.drain()
            v.tensor_scalar(v1q, x1r, cvec[:, 10:11], 0.5, ALU.mult, ALU.add)
            fint = stg.bitcast(I32)[:, 0:2048]
            v.tensor_copy(fint, v1q)
            v.tensor_tensor(sqd, v1q, fint, ALU.is_lt)
            sm["ve"].inc(v.tensor_tensor(b.q1tm[:, :], fint, sqd, ALU.subtract), 1)

    @block.scalar
    def _(s):
        for tt in range(NTT):
            s.wait_ge(sm["ve"].h, s0["ve"] + 4 * tt + 1)
            sm["se"].inc(s.activation(sqd, x1pre, AF.Square, accum_out=cvec[:, 1:2]), 1)
            s.wait_ge(sm["ve"].h, s0["ve"] + 4 * tt + 2)
            sm["se"].inc(s.activation(cvec[:, 5:6], cvec[:, 4:5], AF.Sqrt,
                                      bias=cst["epsv"][:, 1:2]), 1)
            s.wait_ge(sm["ve"].h, s0["ve"] + 4 * tt + 4)
            for i in range(KC):
                sm["vq"].inc(s.dma_start(
                    out=b.mixA[:, i * T + tt * 128: i * T + (tt + 1) * 128],
                    in_=b.q1tm[:, i * 128:(i + 1) * 128], transpose=True), 16)

    scope.close()
    dump(b, "q1", b.mixA[:, :])
    dump(b, "sx1", b.sx1[:, :])


def ph_ffn(b):
    nc, block, sm, cst = b.nc, b.block, b.sm, b.cst
    ffs = ExitStack()
    FB = lambda name, shape, dt: ffs.enter_context(nc.sbuf_tensor(name, shape, dt))
    hbuf = ffs.enter_context(nc.sbuf_tensor("hbuf", [128, MC * T], BF16, side="right"))
    srow = FB("srow", [16, 128], F32)
    fibuf2 = FB("fibuf2", [128, 2048], I32)
    sbsc = ExitStack()
    sbc = sbsc.enter_context(nc.sbuf_tensor("sbc", [128, T], F32))
    gvs = ExitStack()
    gwb = [gvs.enter_context(nc.sbuf_tensor("gwb0", [128, 2048], BF16, side="right")),
           gvs.enter_context(nc.sbuf_tensor("gwb1", [128, 2048], BF16, side="right"))]
    vwb = [gvs.enter_context(nc.sbuf_tensor("vwb0", [128, 2048], BF16, side="right")),
           gvs.enter_context(nc.sbuf_tensor("vwb1", [128, 2048], BF16, side="right"))]
    mxf = b.tmp1.bitcast(BF16)[:, 0:1024]
    mnf = b.tmp1.bitcast(BF16)[:, 1024:2048]
    sbch = sbc
    mixAf = b.mixA.bitcast(F32)
    x2a = mixAf[:, 0:1024]
    x2b = mixAf[:, 1024:2048]
    sqb = mixAf[:, 2048:4096]
    s0 = snap(b)
    scope = ExitStack()
    scopeP1 = ExitStack()
    pB = [psum(b, scopeP1, f"pbc{j}", [128, 512]) for j in range(2)]
    pTs = psum(b, scopeP1, "pts", [8, 128])

    @block.tensor
    def _(t):
        entry_waits(t, b, s0)
        sm["te"].inc(t.transpose(pTs[:, :], b.sx1[:, :], cst["identf"][:, :]), 1)

    @block.vector
    def _(v):
        v.wait_ge(sm["te"].h, s0["te"] + 1)
        sm["ve"].inc(v.tensor_copy(srow[0:8, :], pTs[:, :]), 1)

    @block.gpsimd
    def _(g):
        g.wait_ge(sm["ve"].h, s0["ve"] + 1)
        sm["dga"].inc(g.dma_start(
            out=b.tmp0[0:1, 0:1024].rearrange("o (tt p) -> o tt p", p=128),
            in_=srow[0:8, :]), 16)

    @block.tensor
    def _(t):
        t.wait_ge(sm["dga"].h, s0["dga"] + 16)
        for th in range(2):
            mm = t.matmul(pB[th][:, :], cst["ones"][0:1, 0:128],
                          b.tmp0[0:1, th * 512:(th + 1) * 512], start=True, stop=True)
        sm["te"].inc(mm, 1)

    @block.vector
    def _(v):
        v.wait_ge(sm["te"].h, s0["te"] + 2)
        for th in range(2):
            cp = v.tensor_copy(sbc[:, th * 512:(th + 1) * 512], pB[th][:, :])
        sm["ve"].inc(cp, 1)

    scopeP1.close()
    scopeP2 = ExitStack()
    pg = [psum(b, scopeP2, f"pg{j}", [128, T]) for j in range(2)]
    pv = [psum(b, scopeP2, f"pvf{j}", [128, T]) for j in range(2)]
    s1 = snap(b)

    @block.sync
    def _(sy):
        sy.wait_ge(sm["cc"].h, s1["cc"])
        entry_waits(sy, b, s1)
        for m in range(MC):
            if m >= 2:
                sy.wait_ge(sm["te"].h, s1["te"] + (m - 1))
            csrc, ml = divmod(m, 8)
            off = ml * (128 * 2048)
            src = b.cc_wq_out.ap()[csrc:csrc + 1, off:off + 128 * 2048] \
                .rearrange("o (p kk) -> (o p) kk", p=128)
            (sm["dfa"] if m % 2 == 0 else sm["dfb"]).inc(
                sy.dma_start(out=gwb[m % 2][:, :], in_=src), 16)
            off = 1024 * C + ml * (128 * 2048)
            src = b.cc_wq_out.ap()[csrc:csrc + 1, off:off + 128 * 2048] \
                .rearrange("o (p kk) -> (o p) kk", p=128)
            (sm["dfa"] if m % 2 == 0 else sm["dfb"]).inc(
                sy.dma_start(out=vwb[m % 2][:, :], in_=src), 16)

    @block.tensor
    def _(t):
        t.wait_ge(sm["ve"].h, s1["ve"])
        for m in range(MC):
            par = m % 2
            dwk = "dfa" if par == 0 else "dfb"
            t.wait_ge(sm[dwk].h, s1[dwk] + 32 * (m // 2 + 1))
            if m >= 2:
                t.wait_ge(sm["ve"].h, s1["ve"] + 4 * (m - 1))
            for k in range(KC):
                for th in range(2):
                    t.matmul(pg[par][:, th * 512:(th + 1) * 512],
                             gwb[par][:, k * 128:(k + 1) * 128],
                             b.mixA[:, k * T + th * 512: k * T + (th + 1) * 512],
                             start=(k == 0), stop=(k == KC - 1))
                    mm = t.matmul(pv[par][:, th * 512:(th + 1) * 512],
                                  vwb[par][:, k * 128:(k + 1) * 128],
                                  b.mixA[:, k * T + th * 512: k * T + (th + 1) * 512],
                                  start=(k == 0), stop=(k == KC - 1))
            sm["te"].inc(mm, 1)

    @block.vector
    def _(v):
        for m in range(MC):
            par = m % 2
            v.wait_ge(sm["te"].h, s1["te"] + m + 1)
            for th in range(2):
                hsl = slice(m * T + th * 512, m * T + (th + 1) * 512)
                if m + th > 0:
                    v.wait_ge(sm["se"].h, s1["se"] + 2 * m + th)
                v.scalar_tensor_tensor(b.tmp0[:, 0:512], pg[par][:, th * 512:(th + 1) * 512],
                                       b.thr[:, 0:1], sbc[:, th * 512:(th + 1) * 512],
                                       ALU.mult, ALU.mult)
                sm["ve"].inc(v.scalar_tensor_tensor(
                    b.tmp0[:, 512:1024], pv[par][:, th * 512:(th + 1) * 512],
                    b.thr[:, 1:2], sbc[:, th * 512:(th + 1) * 512],
                    ALU.mult, ALU.mult), 1)
                v.wait_ge(sm["se"].h, s1["se"] + 2 * m + th + 1)
                v.scalar_tensor_tensor(hbuf[:, hsl], b.tmp0[:, 512:1024],
                                       cst["val_b"][:, m:m + 1], b.tmp2[:, 0:512],
                                       ALU.add, ALU.mult)
                if m == 0:
                    v.tensor_copy(mxf[:, th * 512:(th + 1) * 512], hbuf[:, hsl])
                    last = v.tensor_copy(mnf[:, th * 512:(th + 1) * 512], hbuf[:, hsl])
                else:
                    v.tensor_tensor(mxf[:, th * 512:(th + 1) * 512],
                                    mxf[:, th * 512:(th + 1) * 512], hbuf[:, hsl], ALU.max)
                    last = v.tensor_tensor(mnf[:, th * 512:(th + 1) * 512],
                                           mnf[:, th * 512:(th + 1) * 512], hbuf[:, hsl],
                                           ALU.min)
                sm["ve"].inc(last, 1)

    @block.scalar
    def _(s):
        for m in range(MC):
            for th in range(2):
                s.wait_ge(sm["ve"].h, s1["ve"] + 4 * m + 2 * th + 1)
                sm["se"].inc(s.activation(b.tmp2[:, 0:512], b.tmp0[:, 0:512], AF.Silu,
                                          bias=cst["gate_b"][:, m:m + 1]), 1)

    scopeP2.close()
    scopeP3 = ExitStack()
    pQh = [psum(b, scopeP3, f"pqh{j}", [128, 128], BF16) for j in range(2)]
    pTs2 = psum(b, scopeP3, "pts2", [8, 128])
    pB2 = [psum(b, scopeP3, f"pbc2{j}", [128, 512]) for j in range(2)]
    s2 = snap(b)

    @block.vector
    def _(v):
        v.tensor_scalar(b.tmp0[:, 0:1024], mnf[:, :], -1.0, None, ALU.mult)
        sm["ve"].inc(v.tensor_tensor(mxf[:, :], mxf[:, :], b.tmp0[:, 0:1024], ALU.max), 1)

    @block.tensor
    def _(t):
        t.wait_ge(sm["ve"].h, s2["ve"] + 1)
        for j in range(NTT):
            if j >= 2:
                t.wait_ge(sm["ve"].h, s2["ve"] + 1 + (j - 1))
            sm["te"].inc(t.transpose(pQh[j % 2][:, :], mxf[:, j * 128:(j + 1) * 128],
                                     cst["ident"][:, :]), 1)

    @block.vector
    def _(v):
        for j in range(NTT):
            v.wait_ge(sm["te"].h, s2["te"] + j + 1)
            sm["ve"].inc(v.tensor_reduce(b.tmp1[:, 20 + j:21 + j],
                                         pQh[j % 2][:, :], AX.X, ALU.max), 1)
        v.drain()
        v.tensor_scalar(b.tmp1[:, 20:28], b.tmp1[:, 20:28], 1e-5, None, ALU.max)
        v.drain()
        v.reciprocal(b.tmp1[:, 28:36], b.tmp1[:, 20:28])
        v.drain()
        v.tensor_scalar(b.tmp1[:, 36:44], b.tmp1[:, 28:36], 127.0, None, ALU.mult)
        v.tensor_scalar(b.tmp1[:, 44:52], b.tmp1[:, 20:28], 1.0 / 127.0, None, ALU.mult)
        v.drain()
        sm["ve"].inc(v.tensor_scalar(b.s3v[:, :], b.tmp1[:, 44:52], b.thr[:, 2:3],
                                     None, ALU.mult), 1)

    @block.tensor
    def _(t):
        t.wait_ge(sm["ve"].h, s2["ve"] + 1 + NTT + 1)
        sm["te"].inc(t.transpose(pTs2[:, :], b.tmp1[:, 36:44], cst["identf"][:, :]), 1)

    @block.vector
    def _(v):
        v.wait_ge(sm["te"].h, s2["te"] + NTT + 1)
        sm["ve"].inc(v.tensor_copy(srow[0:8, :], pTs2[:, :]), 1)

    @block.gpsimd
    def _(g):
        g.wait_ge(sm["ve"].h, s2["ve"] + NTT + 3)
        sm["dga"].inc(g.dma_start(
            out=b.tmp0[0:1, 0:1024].rearrange("o (tt p) -> o tt p", p=128),
            in_=srow[0:8, :]), 16)

    @block.tensor
    def _(t):
        t.wait_ge(sm["dga"].h, s2["dga"] + 16)
        for th in range(2):
            mm = t.matmul(pB2[th][:, :], cst["ones"][0:1, 0:128],
                          b.tmp0[0:1, th * 512:(th + 1) * 512], start=True, stop=True)
        sm["te"].inc(mm, 1)

    @block.vector
    def _(v):
        v.wait_ge(sm["te"].h, s2["te"] + NTT + 2)
        for th in range(2):
            cp = v.tensor_copy(sbch[:, th * 512:(th + 1) * 512], pB2[th][:, :])
        sm["ve"].inc(cp, 1)

    scopeP3.close()
    s3 = snap(b)

    @block.vector
    def _(v):
        v.wait_ge(sm["ve"].h, s3["ve"])
        for m in range(MC):
            hsl = slice(m * T, (m + 1) * T)
            v.tensor_tensor(b.tmp0[:, :], hbuf[:, hsl], sbch[:, :], ALU.mult)
            v.tensor_scalar(b.tmp1[:, :], b.tmp0[:, :], 0.5, None, ALU.add)
            v.tensor_copy(fibuf2[:, 0:1024], b.tmp1[:, :])
            v.tensor_tensor(b.tmp2[:, :], b.tmp1[:, :], fibuf2[:, 0:1024], ALU.is_lt)
            sm["ve"].inc(v.tensor_tensor(hbuf[:, hsl], fibuf2[:, 0:1024], b.tmp2[:, :],
                                         ALU.subtract), 1)

    gvs.close()
    sbsc.close()
    outsc = ExitStack()
    wob = [outsc.enter_context(nc.sbuf_tensor("wob0", [128, 2048], BF16, side="right")),
           outsc.enter_context(nc.sbuf_tensor("wob1", [128, 2048], BF16, side="right"))]
    x1st0 = outsc.enter_context(nc.sbuf_tensor("x1st0", [128, C], F32, side="right"))
    x1st = [x1st0, x1st0]
    out_b = mixAf[0:1, 6144:8192]
    s4 = snap(b)
    pO = [psum(b, scope, f"po{j}", [128, 512]) for j in range(8)]

    def store_out(sy, tt):
        sy.wait_ge(sm["ve"].h, s4["ve"] + 4 * tt + 3)
        sm["st"].inc(sy.dma_start(out=b.out_hbm[tt * 128:(tt + 1) * 128, 0:1024],
                                  in_=x2a), 16)
        sm["st"].inc(sy.dma_start(out=b.out_hbm[tt * 128:(tt + 1) * 128, 1024:2048],
                                  in_=x2b), 16)

    @block.sync
    def _(sy):
        sy.wait_ge(sm["te"].h, s4["te"])
        entry_waits(sy, b, s4)
        sm["dc"].inc(sy.dma_start(out=out_b.rearrange("o (a c) -> o a c", a=4),
                                  in_=b.di["out_b"][:, :]), 16)
        for tg in range(4):
            for ttl in range(2):
                tt = tg * 2 + ttl
                if tt >= 1:
                    sy.wait_ge(sm["ve"].h, s4["ve"] + 4 * (tt - 1) + 1)
                (sm["dxa"] if tt % 2 == 0 else sm["dxb"]).inc(
                    sy.dma_start(out=x1st[tt % 2][:, :],
                                 in_=b.x1_scr[tt * 128:(tt + 1) * 128, :]), 16)
                if tt >= 1:
                    store_out(sy, tt - 1)
        store_out(sy, NTT - 1)

    @block.gpsimd
    def _(g):
        entry_waits(g, b, s4)
        for tg in range(4):
            for k in range(MC):
                gi = tg * MC + k
                if gi >= 2:
                    g.wait_ge(sm["te"].h, s4["te"] + gi - 1)
                csrc, kl = divmod(k, 8)
                off = 2 * 1024 * C + kl * (128 * C)
                src = b.cc_wq_out.ap()[csrc:csrc + 1, off:off + 128 * C] \
                    .rearrange("o (p j) -> (o p) j", p=128)
                (sm["dga"] if gi % 2 == 0 else sm["dgb"]).inc(
                    g.dma_start(out=wob[gi % 2][:, :], in_=src), 16)

    @block.tensor
    def _(t):
        t.wait_ge(sm["dc"].h, sm["dc"].n)
        for tg in range(4):
            if tg >= 1:
                t.wait_ge(sm["ve"].h, s4["ve"] + 4 * (2 * tg - 1) + 1)
            for ttl in range(2):
                for o in range(4):
                    t.matmul(pO[ttl * 4 + o][:, :], cst["ones"][0:1, 0:128],
                             out_b[0:1, o * 512:(o + 1) * 512],
                             start=True, stop=False)
            for k in range(MC):
                gi = tg * MC + k
                dk = "dga" if gi % 2 == 0 else "dgb"
                t.wait_ge(sm[dk].h, s4[dk] + 16 * (gi // 2 + 1))
                for ttl in range(2):
                    tt = tg * 2 + ttl
                    lhs = hbuf[:, k * T + tt * 128: k * T + (tt + 1) * 128]
                    for o in range(4):
                        mm = t.matmul(pO[ttl * 4 + o][:, :], lhs,
                                      wob[gi % 2][:, o * 512:(o + 1) * 512],
                                      start=False, stop=(k == MC - 1))
                sm["te"].inc(mm, 1)

    @block.vector
    def _(v):
        for tt in range(NTT):
            tg, ttl = divmod(tt, 2)
            v.wait_ge(sm["te"].h, s4["te"] + 64 * (tg + 1))
            v.wait_ge((sm["dxa"] if tt % 2 == 0 else sm["dxb"]).h,
                      s4["dxa" if tt % 2 == 0 else "dxb"] + 16 * (tt // 2 + 1))
            if tt >= 1:
                v.wait_ge(sm["st"].h, s4["st"] + 32 * tt)
            for o in range(4):
                v.tensor_scalar(b.tmp2[:, 0:512], pO[ttl * 4 + o][:, :],
                                b.s3v[:, tt:tt + 1], None, ALU.mult)
                v.tensor_tensor(fibuf2.bitcast(F32)[:, o * 512:(o + 1) * 512],
                                b.tmp2[:, 0:512],
                                x1st[tt % 2][:, o * 512:(o + 1) * 512], ALU.add)
            x2pre = fibuf2.bitcast(F32)[:, 0:2048]
            sm["ve"].inc(v.tensor_reduce(b.tmp1[:, 0:1], x2pre, AX.X, ALU.add), 1)
            v.wait_ge(sm["se"].h, s4["se"] + 2 * tt + 1)
            v.drain()
            v.tensor_scalar(b.tmp1[:, 2:3], b.tmp1[:, 0:1], 1.0 / 2048.0, None, ALU.mult)
            v.tensor_scalar(b.tmp1[:, 3:4], b.tmp1[:, 1:2], 1.0 / 2048.0, None, ALU.mult)
            v.drain()
            v.tensor_tensor(b.tmp1[:, 4:5], b.tmp1[:, 2:3], b.tmp1[:, 2:3], ALU.mult)
            v.drain()
            sm["ve"].inc(v.tensor_tensor(b.tmp1[:, 4:5], b.tmp1[:, 3:4], b.tmp1[:, 4:5],
                                         ALU.subtract), 1)
            v.wait_ge(sm["se"].h, s4["se"] + 2 * tt + 2)
            v.reciprocal(b.tmp1[:, 6:7], b.tmp1[:, 5:6])
            v.drain()
            v.tensor_scalar(x2a, x2pre[:, 0:1024],
                            b.tmp1[:, 2:3], b.tmp1[:, 6:7], ALU.subtract, ALU.mult)
            sm["ve"].inc(v.tensor_scalar(x2b, x2pre[:, 1024:2048],
                                         b.tmp1[:, 2:3], b.tmp1[:, 6:7],
                                         ALU.subtract, ALU.mult), 2)

    @block.scalar
    def _(s):
        for tt in range(NTT):
            s.wait_ge(sm["ve"].h, s4["ve"] + 4 * tt + 1)
            x2pre = fibuf2.bitcast(F32)[:, 0:2048]
            sm["se"].inc(s.activation(sqb, x2pre, AF.Square,
                                      accum_out=b.tmp1[:, 1:2]), 1)
            s.wait_ge(sm["ve"].h, s4["ve"] + 4 * tt + 2)
            sm["se"].inc(s.activation(b.tmp1[:, 5:6], b.tmp1[:, 4:5], AF.Sqrt,
                                      bias=cst["epsv"][:, 1:2]), 1)

    scope.close()
    outsc.close()
    ffs.close()



def prep_host_constants():
    Ms = np.triu(np.ones((L, L), np.float32), 1)
    Md = np.eye(L, dtype=np.float32)
    masksel = np.concatenate([np.tile(Ms, (2, 2)), np.tile(Md, (2, 2))], axis=1)
    e_all = np.zeros((128, KC * 32), np.float32)
    f_all = np.zeros((32, KC * 128), np.float32)
    for i in range(KC):
        for p in range(128):
            h = 2 * i + p // 64
            e_all[p, i * 32 + h] = 1.0
            f_all[h, i * 128 + p] = 1.0
    import ml_dtypes
    bf = ml_dtypes.bfloat16
    return dict(
        ident_bf=np.eye(128, dtype=np.float32).astype(bf),
        ident_f32=np.eye(128, dtype=np.float32),
        masksel=masksel.astype(np.float32),
        maskdiag_u8=np.tile(np.eye(64, dtype=np.uint8), (2, 2)),
        e_all=e_all.astype(bf),
        f_all=f_all.astype(bf),
        ones_f32=np.ones((128, 512), np.float32),
        epsv=np.tile(np.array([[1e-5 * 64.0, 1e-5]], np.float32), (128, 1)),
    )


def vec_fm(v):
    return np.ascontiguousarray(np.asarray(v, np.float32).reshape(KC, 128).T)


def wslab(WT):
    return np.ascontiguousarray(
        WT.reshape(KC, 128, KC, 128).transpose(2, 1, 0, 3).reshape(C, C))


def prep_inputs(ins, core):
    f32 = np.float32
    x = np.asarray(ins["L0_attn_x"], f32)[0]
    sl = slice(core * T, (core + 1) * T)
    x_sl = x[sl]
    xprev = np.zeros((C,), f32) if core == 0 else x[core * T - 1]
    m2aug = np.zeros((9, 8), f32)
    for bcore in range(8):
        if bcore < core:
            for e in range(bcore + 1, core):
                m2aug[e, bcore] = 1.0
        else:
            m2aug[8, bcore] = 1e9
    gate_wT = np.asarray(ins["L0_ffn_gate_w"], f32).T
    val_wT = np.asarray(ins["L0_ffn_val_w"], f32).T
    out_wT = np.asarray(ins["L0_ffn_out_w"], f32).T
    maas = np.zeros((128, KC * 6), f32)
    for j, nm in enumerate("xwkvrg"):
        v = vec_fm(ins[f"L0_attn_time_maa_{nm}"])
        for i in range(KC):
            maas[:, i * 6 + j] = v[:, i]
    d = dict(
        x_fm=np.ascontiguousarray(x_sl.T),
        xprev=vec_fm(xprev),
        x_tm=np.ascontiguousarray(x_sl),
        WrL=wslab(np.asarray(ins["L0_attn_W_r"], f32).T),
        WkL=wslab(np.asarray(ins["L0_attn_W_k"], f32).T),
        WvT=np.ascontiguousarray(np.asarray(ins["L0_attn_W_v"], f32).T),
        WgL=wslab(np.asarray(ins["L0_attn_W_g"], f32).T),
        WoT=np.ascontiguousarray(np.asarray(ins["L0_attn_W_o"], f32).T),
        maa_w1L=np.ascontiguousarray(
            np.asarray(ins["L0_attn_time_maa_w1"], f32).reshape(KC, 128, 160)
            .transpose(1, 0, 2).reshape(128, KC * 160)),
        maa_w2=np.asarray(ins["L0_attn_time_maa_w2"], f32),
        dw1L=np.ascontiguousarray(
            np.asarray(ins["L0_attn_time_decay_w1"], f32).reshape(KC, 128, 64)
            .transpose(1, 0, 2).reshape(128, KC * 64)),
        dw2=np.asarray(ins["L0_attn_time_decay_w2"], f32),
        maa_vecs=maas,
        td_vec=vec_fm(ins["L0_attn_time_decay"]),
        u_vec=vec_fm(np.asarray(ins["L0_attn_time_faaaa"], f32).reshape(-1)),
        lng=vec_fm(ins["L0_attn_ln_gamma"]),
        lnb=vec_fm(ins["L0_attn_ln_beta"]),
        m2aug=m2aug,
        gate_slice=np.ascontiguousarray(gate_wT[:, core * 1024:(core + 1) * 1024]),
        val_slice=np.ascontiguousarray(val_wT[:, core * 1024:(core + 1) * 1024]),
        out_slice=np.ascontiguousarray(out_wT[core * 1024:(core + 1) * 1024, :]),
        gate_b=vec_fm_h(ins["L0_ffn_gate_b"]),
        val_b=vec_fm_h(ins["L0_ffn_val_b"]),
        out_b=np.asarray(ins["L0_ffn_out_b"], f32).reshape(4, 512),
    )
    d.update(prep_host_constants())
    return d


def vec_fm_h(v):
    return np.ascontiguousarray(np.asarray(v, np.float32).reshape(MC, 128).T)


def run(ins, stop_after=None, dumps=(), cores=NCORES):
    from concourse.bass_utils import run_bass_kernel_spmd
    nc, b = build(stop_after=stop_after, dumps=dumps)
    in_maps = [prep_inputs(ins, c) for c in range(cores)]
    res = run_bass_kernel_spmd(nc, in_maps, core_ids=list(range(cores)))
    return res, b


def kernel(**inputs):
    from concourse.bass_utils import run_bass_kernel_spmd
    nc, b = build()
    in_maps = [prep_inputs(inputs, c) for c in range(NCORES)]
    res = run_bass_kernel_spmd(nc, in_maps, core_ids=list(range(NCORES)))
    out = np.concatenate([res.results[c]["out"] for c in range(NCORES)], axis=0)
    return out.reshape(1, 8192, C)
